# version 11
# baseline (speedup 1.0000x reference)
import numpy as np
import ml_dtypes
import concourse.bass as bass
import concourse.mybir as mybir
from concourse.bass_utils import run_bass_kernel_spmd

F32 = mybir.dt.float32
BF16 = mybir.dt.bfloat16
AF = mybir.ActivationFunctionType
ALU = mybir.AluOpType
AX = mybir.AxisListType
BF = ml_dtypes.bfloat16

D = 1024
NC = 8
CHAINED = {"vector": "s_dv", "scalar": "s_ac", "gpsimd_c": "s_gc"}
ENGMAP = {"vector": "vector", "scalar": "scalar", "gpsimd_c": "gpsimd",
          "tensor": "tensor", "sync": "sync", "gpsimd": "gpsimd",
          "scalarq": "scalar"}


class Prog:

    def __init__(self):
        self.steps = []
        self._dmacnt = {}
        self._rr = {}

    def op(self, stream, fn, waits=(), incs=(), marks=()):
        self.steps.append(dict(stream=stream, fn=fn, waits=list(waits),
                               incs=list(incs), marks=list(marks), selfwait=0))

    def dma(self, stream, out, in_, sem, waits=(), marks=(), lanes=4):
        lane = self._rr.get(sem, 0)
        self._rr[sem] = (lane + 1) % lanes
        s = f"{sem}{lane}"
        cur = self._dmacnt.get(s, 0)
        self._dmacnt[s] = cur + 16
        w = list(waits) + ([("val", s, cur)] if cur else [])
        self.op(stream, lambda e, o=out, i=in_: e.dma_start(out=o, in_=i),
                waits=w, incs=[(s, 16)], marks=marks)

    def resolve(self):
        cnt = {}
        self.markv = {}
        for s in self.steps:
            sname = CHAINED.get(s["stream"])
            if sname is not None:
                s["selfwait"] = cnt.get(sname, 0)
                s["incs"] = s["incs"] + [(sname, 1)]
            for semn, n in s["incs"]:
                cnt[semn] = cnt.get(semn, 0) + (n if n else 1)
            for mk in s["marks"]:
                semn = s["incs"][-1][0] if s["incs"] else None
                assert semn is not None, f"mark {mk} on step without inc"
                self.markv[mk] = (semn, cnt[semn])

    def emit(self, engines, sems):
        for s in self.steps:
            eng = engines[ENGMAP[s["stream"]]]
            for w in s["waits"]:
                if isinstance(w, tuple) and w[0] == "val":
                    if w[2] > 0:
                        eng.wait_ge(sems[w[1]], w[2])
                else:
                    semn, v = self.markv[w]
                    if v > 0:
                        eng.wait_ge(sems[semn], v)
            sname = CHAINED.get(s["stream"])
            if sname is not None and s["selfwait"] > 0:
                eng.wait_ge(sems[sname], s["selfwait"])
            inst = s["fn"](eng)
            for semn, n in s["incs"]:
                if n:
                    inst.then_inc(sems[semn], n)
                else:
                    inst.then_inc(sems[semn])


def build_program(TP=4096):
    NT = TP // 128
    UC = TP // 16
    QCP = (2 * UC) // 128
    NTC = NT
    assert QCP * 8 == NTC and UC % 128 == 0

    nc = bass.Bass("TRN2", target_bir_lowering=False, debug=False)

    xtok_d = nc.dram_tensor("xtok", [TP, D], F32, kind="ExternalInput").ap()
    xout_d = nc.dram_tensor("xout", [TP, D], F32, kind="ExternalInput").ap()
    wq_d = nc.dram_tensor("wq", [D, D], F32, kind="ExternalInput").ap()
    wk_d = nc.dram_tensor("wk", [D, D], F32, kind="ExternalInput").ap()
    wvT_d = nc.dram_tensor("wvT", [D, D], F32, kind="ExternalInput").ap()
    wo_d = nc.dram_tensor("wo", [D, D], F32, kind="ExternalInput").ap()
    wT_d = nc.dram_tensor("wT", [D, D], F32, kind="ExternalInput").ap()
    luT_d = nc.dram_tensor("luT", [4, 50], F32, kind="ExternalInput").ap()
    ld_d = nc.dram_tensor("ld", [4, 100], F32, kind="ExternalInput").ap()
    uaT_d = nc.dram_tensor("uaT", [50, D], F32, kind="ExternalInput").ap()
    da_d = nc.dram_tensor("da", [100, D], F32, kind="ExternalInput").ap()
    cie_d = nc.dram_tensor("cie", [16, 2048], BF16, kind="ExternalInput").ap()
    bias2_d = nc.dram_tensor("bias2", [1, 2048], BF16, kind="ExternalInput").ap()
    imask_d = nc.dram_tensor("imask", [128, D], BF16, kind="ExternalInput").ap()
    eye_d = nc.dram_tensor("eye", [128, 128], BF16, kind="ExternalInput").ap()
    eye16_d = nc.dram_tensor("eye16", [16, 16], F32, kind="ExternalInput").ap()
    sel2_d = nc.dram_tensor("sel2", [128, 2], F32, kind="ExternalInput").ap()
    y_d = nc.dram_tensor("y", [TP, D], F32, kind="ExternalOutput").ap()
    cc_in = nc.dram_tensor("cc_in", [16, 16], F32)
    cc_out = nc.dram_tensor("cc_out", [16, 16], F32)

    gidx = {}
    for ii in range(8):
        for jj in range(ii, 8):
            gidx[(ii, jj)] = len(gidx)

    from contextlib import ExitStack
    with ExitStack() as stk:
        e = stk.enter_context
        big0 = e(nc.sbuf_tensor("big0", [128, max(NT, 32) * D], BF16))
        big1 = e(nc.sbuf_tensor("big1", [128, 8, TP], BF16))
        wb0 = e(nc.sbuf_tensor("wb0", [128, 8, D], BF16))
        wb1 = e(nc.sbuf_tensor("wb1", [128, 8, D], BF16))
        wb2 = e(nc.sbuf_tensor("wb2", [128, 8, D], BF16))
        Gp = e(nc.sbuf_tensor("Gp", [128, 36, 128], BF16))
        x_raw = e(nc.sbuf_tensor("x_raw", [128, 4, D], F32))
        eye = e(nc.sbuf_tensor("eye_s", [128, 128], BF16))
        eye16 = e(nc.sbuf_tensor("eye16_s", [16, 16], F32))
        sel2 = e(nc.sbuf_tensor("sel2_s", [128, 2], F32))
        ones1 = e(nc.sbuf_tensor("ones1", [1, 128], BF16))
        lu_s = e(nc.sbuf_tensor("lu_s", [4, 50], BF16))
        ld_s = e(nc.sbuf_tensor("ld_s", [4, 100], BF16))
        st6 = e(nc.sbuf_tensor("st6", [128, 12], F32))
        st2 = e(nc.sbuf_tensor("st2", [128, 2], F32))
        nmean = e(nc.sbuf_tensor("nmean", [128, 2], F32))
        pwin = e(nc.sbuf_tensor("pwin", [128, 2], F32))
        pwex = e(nc.sbuf_tensor("pwex", [128, 1], F32))
        rstd_g = e(nc.sbuf_tensor("rstd_g", [128, 2], F32))
        ET_sb = e(nc.sbuf_tensor("ET_sb", [16, 16], F32))
        ETs_sb = e(nc.sbuf_tensor("ETs_sb", [16, 16], F32))
        E_sb = e(nc.sbuf_tensor("E_sb", [16, 16], F32))
        A_sb = e(nc.sbuf_tensor("A_sb", [16, 16], F32))
        Ab_sb = e(nc.sbuf_tensor("Ab_sb", [16, 16], BF16))
        sm1 = e(nc.sbuf_tensor("sm1", [16, 1], F32))
        sm2 = e(nc.sbuf_tensor("sm2", [16, 1], F32))
        R_sb = e(nc.sbuf_tensor("R_sb", [128, 16], F32))

        ps_b0 = e(nc.psum_tensor("ps_b0", [128, D], F32))
        ps_b1 = e(nc.psum_tensor("ps_b1", [128, D], F32))
        ps_t0 = e(nc.psum_tensor("ps_t0", [128, 8, 128], BF16))
        ps_t1 = e(nc.psum_tensor("ps_t1", [128, 8, 128], BF16))
        ps_m0 = e(nc.psum_tensor("ps_m0", [128, 512], F32))
        ps_m1 = e(nc.psum_tensor("ps_m1", [128, 512], F32))
        ps_b = [ps_b0, ps_b1]
        ps_t = [ps_t0, ps_t1]
        ps_m = [ps_m0, ps_m1]

        dma_bases = ["s_xl", "s_yst", "s_ccb", "s_wl", "s_shf"]
        sem_names = (["s_dv", "s_ac", "s_pe", "s_gc", "s_cc"]
                     + [f"{b}{i}" for b in dma_bases for i in range(4)])
        sems = {n: e(nc.semaphore(n)) for n in sem_names}

        def B0(lo, n):
            return big0[:, lo:lo + n]
        xn_tok = lambda k: B0(k * D, D)
        H_sb = B0(0, 8 * D).rearrange("p (c n) -> p c n", c=8)
        Fm = B0(8 * D, 2 * D).bitcast(F32)
        mscr = B0(10 * D, 2 * 7 * 128).rearrange("p (b c n) -> p b c n", b=2, c=7)
        Arep_sb = B0(0, D)
        xn2_t = [B0(D, D), B0(22 * D, D)]
        g_pool = B0(2 * D, 2 * D).rearrange("p (s n) -> p s n", s=2)
        y_buf = [B0(4 * D, 2 * D).bitcast(F32), B0(6 * D, 2 * D).bitcast(F32)]
        cie = B0(13 * D, 2 * D)
        imask = B0(12 * D, D)
        slab = B0(8 * D, 16 * UC)
        opT = B0(12 * D, TP)
        s_tile = [B0(16 * D, D), B0(17 * D, D)]
        p_tile = [B0(18 * D, D), B0(19 * D, D)]
        slabT = B0(20 * D, 2 * 8 * 128).rearrange("p (b c n) -> p b c n", b=2, c=8)
        M_big = B0(4 * D, 8 * D).rearrange("p (c n) -> p c n", c=8)
        WvE = B0(24 * D, 8 * D).rearrange("p (c n) -> p c n", c=8)

        P = Prog()
        ps_last = {}

        def pe_mm(ps_ap, psname, lhsT, rhs, start, stop, marks=(), waits=()):
            w = list(waits)
            if start and ps_last.get(psname):
                w.append(ps_last[psname])
                ps_last[psname] = None
            P.op("tensor", lambda t, o=ps_ap, l=lhsT, r=rhs, s1=start, s2=stop:
                 t.matmul(o, l, r, start=s1, stop=s2),
                 waits=w, incs=([("s_pe", 1)] if marks else []), marks=marks)

        def pe_tr(ps_ap, psname, in_, ident, start_bank, marks=(), waits=()):
            w = list(waits)
            if start_bank and ps_last.get(psname):
                w.append(ps_last[psname])
                ps_last[psname] = None
            P.op("tensor", lambda t, o=ps_ap, i=in_, d=ident: t.transpose(o, i, d),
                 waits=w, incs=([("s_pe", 1)] if marks else []), marks=marks)

        def dve(fn, waits=(), marks=()):
            P.op("vector", fn, waits=waits, marks=marks)

        def act(fn, waits=(), marks=()):
            P.op("scalar", fn, waits=waits, marks=marks)

        def gdve(fn, waits=(), marks=()):
            P.op("gpsimd_c", fn, waits=waits, marks=marks)

        def gload(out, in_, name, waits=()):
            P.dma("gpsimd", out, in_, "s_wl", waits=waits, marks=[f"wl_{name}"])
        gload(eye[:, :], eye_d[:, :], "eye")
        gload(eye16[:, :], eye16_d[:, :], "eye16")
        gload(sel2[:, :], sel2_d[:, :], "sel2")
        gload(lu_s[:, :], luT_d[:, :], "lu")
        gload(ld_s[:, :], ld_d[:, :], "ld")
        bias_s = Gp[0:1, 0:16, 0:128].rearrange("p a b -> p (a b)")
        ua_s = Gp[0:50, 0:8, 0:128].rearrange("p a b -> p (a b)")
        da_s = Gp[0:100, 8:16, 0:128].rearrange("p a b -> p (a b)")
        r1_s = Gp[0:100, 16:24, 0:128].rearrange("p a b -> p (a b)")
        s1_s = Gp[0:50, 24, 0:100]
        gload(ua_s, uaT_d[:, :], "ua")
        gload(da_s, da_d[:, :], "da")
        P.op("gpsimd_c", lambda g: g.memset(ones1[:, :], 1.0))
        P.op("gpsimd_c", lambda g: g.memset(pwex[:, :], -0.5), marks=["ms_pw"])
        gload(wb1[:, :, :], wT_d.rearrange("(c p) n -> p c n", p=128), "wT")
        gload(wb0[:, :, :], wk_d.rearrange("(c p) n -> p c n", p=128), "wk")
        gload(wb2[:, :, :], wq_d.rearrange("(c p) n -> p c n", p=128), "wq")

        def ln_stats(src, tag):
            dve(lambda v, src=src: v.bn_stats(st6[:, 0:6], src[:, 0:512]),
                waits=[f"xld_{tag}"])
            dve(lambda v, src=src: v.bn_stats(st6[:, 6:12], src[:, 512:1024]))
            dve(lambda v: v.bn_aggr(st2[:, :], st6[:, :]))
            par = tag % 2
            w = [f"xn_{tag - 2}"] if tag >= 2 else []
            dve(lambda v, par=par: v.tensor_scalar(nmean[:, par:par + 1],
                                                   st2[:, 0:1], -1.0, None, ALU.mult),
                waits=w)
            dve(lambda v, par=par: v.tensor_scalar(pwin[:, par:par + 1],
                                                   st2[:, 1:2], 1e-5, None, ALU.add),
                marks=[f"sqin_{tag}"])
            gw = [f"sqin_{tag}"]
            if tag == 0:
                gw.append("ms_pw")
            gdve(lambda g, par=par: g.tensor_tensor(rstd_g[:, par:par + 1],
                                                    pwin[:, par:par + 1],
                                                    pwex[:, :], ALU.pow),
                 waits=gw, marks=[f"pw_{tag}"])

        def ln_norm(dst, src, tag, war_wait=None):
            par = tag % 2
            wn = [f"pw_{tag}"] + ([war_wait] if war_wait else [])
            dve(lambda v, dst=dst, src=src, par=par: v.tensor_scalar(
                dst, src, nmean[:, par:par + 1], rstd_g[:, par:par + 1],
                ALU.add, ALU.mult),
                waits=wn, marks=[f"xn_{tag}"])

        xli = 0
        def xload(dram, k, waits=()):
            nonlocal xli
            P.dma("sync", x_raw[:, xli % 4, :], dram[k * 128:(k + 1) * 128, :],
                  "s_xl", waits=waits, marks=[f"xld_{xli}"])
            xli += 1

        def xload_pair(dram, k, waits=()):
            nonlocal xli
            assert xli % 4 == k % 4 and k % 2 == 0
            P.dma("sync", x_raw[:, k % 4:k % 4 + 2, :],
                  dram[k * 128:(k + 2) * 128, :].rearrange("(t p) c -> p t c", p=128),
                  "s_xl", waits=waits, marks=[f"xld_{xli}", f"xld_{xli + 1}"])
            xli += 2
        xload_pair(xtok_d, 0)
        xload_pair(xtok_d, 2)
        for k in range(4, NT, 2):
            xload_pair(xtok_d, k, waits=[f"xn_{k - 3}"])
        xload_pair(xout_d, 0, waits=[f"xn_{NT - 3}"])
        xload_pair(xout_d, 2, waits=[f"xn_{NT - 1}"])

        def li_step(j):
            if j == 0:
                pe_mm(ps_m0[0:50, 0:100], "m0", lu_s[:, :], ld_s[:, :], True, True,
                      marks=["li_s1"],
                      waits=["wl_lu", "wl_ld", "wl_ua", "wl_da", ("val", "s_gc", 1)])
            elif j == 1:
                dve(lambda v: v.tensor_copy(s1_s, ps_m0[0:50, 0:100]),
                    waits=["li_s1"], marks=["li_s1e"])
                ps_last["m0"] = "li_s1e"
            elif j in (2, 3):
                nb = j - 2
                pn = "m0" if nb == 0 else "m1"
                pe_mm(ps_m[nb][0:100, :], pn, s1_s,
                      ua_s[:, nb * 512:(nb + 1) * 512], True, True,
                      marks=[f"li_r1_{nb}"], waits=(["li_s1e"] if nb == 0 else []))
            elif j in (4, 5):
                nb = j - 4
                dve(lambda v, nb=nb: v.tensor_copy(r1_s[:, nb * 512:(nb + 1) * 512],
                                                   ps_m[nb][0:100, :]),
                    waits=[f"li_r1_{nb}"], marks=[f"li_r1e_{nb}"])
                ps_last["m0" if nb == 0 else "m1"] = f"li_r1e_{nb}"
            elif 6 <= j < 22:
                ic, nb = divmod(j - 6, 2)
                par = (2 * ic + nb) % 2
                pn = "m0" if par == 0 else "m1"
                psx = ps_m[par]
                pe_mm(psx[:, :], pn, da_s[:, ic * 128:(ic + 1) * 128],
                      r1_s[:, nb * 512:(nb + 1) * 512], True, True,
                      marks=[f"lilo_{ic}_{nb}"],
                      waits=(["li_r1e_1", "wl_wT"] if (ic == 0 and nb == 0) else []))
                dve(lambda v, ic=ic, nb=nb, psx=psx:
                    v.tensor_tensor(wb1[:, ic, nb * 512:(nb + 1) * 512],
                                    wb1[:, ic, nb * 512:(nb + 1) * 512],
                                    psx[:, :], ALU.add),
                    waits=[f"lilo_{ic}_{nb}"], marks=[f"weff_{ic}_{nb}"])
                ps_last[pn] = f"weff_{ic}_{nb}"
        WEFF_DONE = "weff_7_1"

        def emit_tr(k):
            pn = "t0" if k % 2 == 0 else "t1"
            for c in range(8):
                pe_tr(ps_t[k % 2][:, c, :], pn,
                      xn_tok(k)[:, c * 128:(c + 1) * 128], eye[:, :],
                      start_bank=(c == 0),
                      waits=([f"xn_{k}", "wl_eye"] if c == 0 else []),
                      marks=([f"trA_{k}"] if c == 7 else []))
            act(lambda s, k=k: s.activation(big1[:, :, k * 128:(k + 1) * 128],
                                            ps_t[k % 2][:, :, :], AF.Copy),
                waits=[f"trA_{k}"], marks=[f"trAe_{k}"])
            ps_last[pn] = f"trAe_{k}"

        def g_chunks(i):
            w_total = D - 128 * i
            chunks = []
            off = 0
            while off < w_total:
                chunks.append((off, min(512, w_total - off)))
                off += min(512, w_total - off)
            return chunks

        def emit_G_s(i, s, wait_xn):
            par = i % 2
            pn = "b0" if par == 0 else "b1"
            psg = ps_b[par]
            chunks = g_chunks(i)
            for ci, (off, w) in enumerate(chunks):
                wt = [f"xn_{s}"] if (ci == 0 and wait_xn) else []
                pe_mm(psg[:, off:off + w], pn,
                      xn_tok(s)[:, i * 128:(i + 1) * 128],
                      xn_tok(s)[:, 128 * i + off:128 * i + off + w],
                      start=(s == 0), stop=(s == NT - 1),
                      waits=wt,
                      marks=([f"G_{i}"] if (s == NT - 1 and ci == len(chunks) - 1) else []))

        def emit_G_evac(i):
            par = i % 2
            pn = "b0" if par == 0 else "b1"
            psg = ps_b[par]
            for jj in range(i, 8):
                act(lambda s, i=i, jj=jj, psg=psg:
                    s.activation(Gp[:, gidx[(i, jj)], :],
                                 psg[:, 128 * (jj - i):128 * (jj - i + 1)], AF.Copy),
                    waits=([f"G_{i}"] if jj == i else []),
                    marks=([f"Ge_{i}"] if jj == 7 else []))
            ps_last[pn] = f"Ge_{i}"

        for k in range(NT):
            ln_stats(x_raw[:, k % 4, :], k)
            if k < 22:
                li_step(k)
            if k >= 1:
                ln_norm(xn_tok(k - 1), x_raw[:, (k - 1) % 4, :], k - 1)
                emit_tr(k - 1)
                emit_G_s(0, k - 1, wait_xn=True)
                emit_G_s(1, k - 1, wait_xn=True)
        for j in range(min(NT, 22), 22):
            li_step(j)
        ln_norm(xn_tok(NT - 1), x_raw[:, (NT - 1) % 4, :], NT - 1)
        emit_tr(NT - 1)
        emit_G_s(0, NT - 1, wait_xn=True)
        emit_G_s(1, NT - 1, wait_xn=True)
        emit_G_evac(0)
        emit_G_evac(1)
        for i in range(2, 8):
            for s in range(NT):
                emit_G_s(i, s, wait_xn=(s == 0 and i == 2))
            emit_G_evac(i)

        gload(imask, imask_d[:, :], "imask", waits=[f"G_{7}"])

        def mirror_tr(i):
            par = i % 2
            pn = "t0" if par == 0 else "t1"
            nblk = 7 - i
            if nblk == 0:
                return
            for bi, J in enumerate(range(i + 1, 8)):
                pe_tr(ps_t[par][:, bi, :], pn, Gp[:, gidx[(i, J)], :], eye[:, :],
                      start_bank=(bi == 0),
                      waits=([f"Ge_{i}"] if bi == 0 else []),
                      marks=([f"GmT_{i}"] if bi == nblk - 1 else []))
            act(lambda s, i=i, par=par, nblk=nblk:
                s.activation(mscr[:, par, 0:nblk, :], ps_t[par][:, 0:nblk, :], AF.Copy),
                waits=[f"GmT_{i}"], marks=[f"GmE_{i}"])
            ps_last[pn] = f"GmE_{i}"
        mirror_tr(0)
        for i in range(8):
            if i + 1 < 8:
                mirror_tr(i + 1)
            for nb in range(2):
                par = (2 * i + nb) % 2
                pn = "b0" if par == 0 else "b1"
                psh = ps_b[par]
                for J in range(8):
                    lhsT = (Gp[:, gidx[(J, i)], :] if J <= i
                            else mscr[:, i % 2, J - i - 1, :])
                    wt = []
                    if nb == 0 and J == 0:
                        wt = ([f"GmE_{i}", "wl_wk"] if i < 7 else ["wl_wk"])
                    pe_mm(psh[:, 0:512], pn, lhsT,
                          wb0[:, J, nb * 512:(nb + 1) * 512],
                          start=(J == 0), stop=(J == 7),
                          waits=wt, marks=([f"H_{i}_{nb}"] if J == 7 else []))
                act(lambda s, i=i, nb=nb, psh=psh:
                    s.activation(H_sb[:, i, nb * 512:(nb + 1) * 512], psh[:, 0:512],
                                 AF.Copy),
                    waits=[f"H_{i}_{nb}"], marks=[f"He_{i}_{nb}"])
                ps_last[pn] = f"He_{i}_{nb}"
        H_ALL = "H_7_1"
        gload(wb0[:, :, :], wo_d.rearrange("(c p) n -> p c n", p=128), "wo",
              waits=[H_ALL])
        gload(bias_s, bias2_d[:, :], "bias2")

        for al in range(8):
            par = al % 2
            pn = "b0" if par == 0 else "b1"
            psf = ps_b[par]
            for i in range(8):
                for fb in range(2):
                    pe_mm(psf[:, fb * 512:(fb + 1) * 512], pn,
                          wb2[:, i, al * 128:(al + 1) * 128],
                          H_sb[:, i, fb * 512:(fb + 1) * 512],
                          start=(i == 0), stop=(i == 7),
                          waits=(["He_7_1", "wl_wq"]
                                 if (al == 0 and i == 0 and fb == 0) else []),
                          marks=([f"F_{al}"] if (i == 7 and fb == 1) else []))
            dve(lambda v, psf=psf: v.tensor_tensor(Fm, psf[:, :], imask, ALU.mult),
                waits=([f"F_{al}", "wl_imask"] if al == 0 else [f"F_{al}"]))
            dve(lambda v: v.tensor_reduce(R_sb[:, :],
                                          Fm.rearrange("p (a b) -> p a b", a=16),
                                          AX.X, ALU.add),
                marks=[f"R_{al}"])
            ps_last[pn] = f"R_{al}"
            pe_mm(ps_m0[0:16, 2 * al:2 * al + 2], "m0", R_sb[:, :], sel2[:, :],
                  start=(al == 0), stop=(al == 7),
                  waits=([f"R_{al}", "wl_sel2"] if al == 0 else [f"R_{al}"]),
                  marks=[f"ETmm_{al}"])
        dve(lambda v: v.tensor_copy(ET_sb[:, :], ps_m0[0:16, 0:16]),
            waits=["ETmm_7"], marks=["ET_evac"])
        ps_last["m0"] = "ET_evac"

        P.dma("sync", cc_in[:, :], ET_sb[:, :], "s_ccb", waits=["ET_evac"],
              marks=["ccb_in"])
        P.op("gpsimd", lambda g: g.collective_compute(
            "AllReduce", ALU.add, replica_groups=[[0, 1, 2, 3], [4, 5, 6, 7]],
            ins=[cc_in.ap().opt()], outs=[cc_out.ap().opt()]),
            waits=["ccb_in"], incs=[("s_cc", None)])
        P.dma("sync", ETs_sb[:, :], cc_out[:, :], "s_ccb",
              waits=[("val", "s_cc", 1)], marks=["ccb_out"])
        gload(wb2[:, :, :], wvT_d.rearrange("(c p) n -> p c n", p=128), "wvT",
              waits=["ETmm_7"])
        gload(cie[0:16, :], cie_d[:, :], "cie", waits=["ETmm_7"])

        def ln2_stats(T):
            ln_stats(x_raw[:, (NT + T) % 4, :], NT + T)

        def ln2_norm_tr(T):
            tag = NT + T
            war = f"trF_{T - 2}" if T >= 2 else None
            ln_norm(xn2_t[T % 2], x_raw[:, tag % 4, :], tag, war_wait=war)
            parT = T % 2
            pnT = "t0" if parT == 0 else "t1"
            for c in range(8):
                pe_tr(ps_t[parT][:, c, :], pnT, xn2_t[T % 2][:, c * 128:(c + 1) * 128],
                      eye[:, :], start_bank=(c == 0),
                      waits=([f"xn_{tag}"] if c == 0 else []),
                      marks=([f"trF_{T}"] if c == 7 else []))
            ew = [f"trF_{T}"]
            if T >= 2:
                ew.append(f"ffn_{T - 2}_1")
            act(lambda s, parT=parT: s.activation(slabT[:, parT, :, :],
                                                  ps_t[parT][:, :, :], AF.Copy),
                waits=ew, marks=[f"slabTe_{T}"])
            ps_last[pnT] = f"slabTe_{T}"

        def ffn_mm(T):
            parT = T % 2
            for ob in range(2):
                half = ps_b0[:, ob * 512:(ob + 1) * 512]
                pn = f"b0h{ob}"
                pe_mm(half, pn, ones1[:, :],
                      bias_s[:, ob * 512:(ob + 1) * 512],
                      start=True, stop=False,
                      waits=([f"slabTe_{T}", WEFF_DONE, "wl_bias2"] if ob == 0 else []))
                for ic in range(8):
                    pe_mm(half, pn, slabT[:, parT, ic, :],
                          wb1[:, ic, ob * 512:(ob + 1) * 512],
                          start=False, stop=(ic == 7),
                          marks=([f"ffn_{T}_{ob}"] if ic == 7 else []))
                wsil = [f"ffn_{T}_{ob}"]
                if T >= 2:
                    wsil.append(f"p_{T - 2}")
                act(lambda s, T=T, ob=ob: s.activation(
                    g_pool[:, T % 2, ob * 512:(ob + 1) * 512],
                    ps_b0[:, ob * 512:(ob + 1) * 512], AF.Silu),
                    waits=wsil, marks=[f"g_{T}_{ob}"])
                ps_last[pn] = f"g_{T}_{ob}"

        ln2_stats(0)
        ln2_norm_tr(0)
        ln2_stats(1)
        ln2_norm_tr(1)
        ffn_mm(0)
        ln2_stats(2)
        ln2_norm_tr(2)
        ffn_mm(1)
        ln2_stats(3)
        ln2_norm_tr(3)

        pe_tr(ps_m1[0:16, 0:16], "m1", ETs_sb[:, :], eye16[:, :], start_bank=True,
              waits=["ccb_out", "wl_eye16"], marks=["Etr"])
        dve(lambda v: v.tensor_copy(E_sb[:, :], ps_m1[0:16, 0:16]), waits=["Etr"],
            marks=["Etr_e"])
        ps_last["m1"] = "Etr_e"
        dve(lambda v: v.reduce_max(sm1[:, :], E_sb[:, :], AX.X))
        dve(lambda v: v.tensor_scalar(sm2[:, :], sm1[:, :], -1.0, None, ALU.mult))
        dve(lambda v: v.tensor_scalar(E_sb[:, :], E_sb[:, :], sm2[:, :], 0.125,
                                      ALU.add, ALU.mult), marks=["smx_in"])
        act(lambda s: s.activation(A_sb[:, :], E_sb[:, :], AF.Exp),
            waits=["smx_in"], marks=["smx_out"])
        dve(lambda v: v.reduce_sum(sm1[:, :], A_sb[:, :], AX.X), waits=["smx_out"])
        dve(lambda v: v.reciprocal(sm2[:, :], sm1[:, :]))
        dve(lambda v: v.tensor_scalar(A_sb[:, :], A_sb[:, :], sm2[:, :], None, ALU.mult))
        dve(lambda v: v.tensor_copy(Ab_sb[:, :], A_sb[:, :]), marks=["A_bf"])
        for nb in range(2):
            pe_mm(ps_m0[0:16, 0:512], "m0", Ab_sb[:, :],
                  cie[0:16, nb * 512:(nb + 1) * 512],
                  start=True, stop=True,
                  waits=(["A_bf", "wl_cie"] if nb == 0 else []),
                  marks=[f"Arep_{nb}"])
            dve(lambda v, nb=nb: v.tensor_copy(Arep_sb[0:16, nb * 512:(nb + 1) * 512],
                                               ps_m0[0:16, 0:512]),
                waits=([f"Arep_{nb}"] + (["He_7_1"] if nb == 0 else [])),
                marks=[f"ArepE_{nb}"])
            ps_last["m0"] = f"ArepE_{nb}"
        for t in range(8):
            par = t % 2
            pn = "b0" if par == 0 else "b1"
            psx = ps_b[par]
            for nb in range(2):
                pe_mm(psx[:, nb * 512:(nb + 1) * 512], pn,
                      cie[0:16, 1024 + t * 128:1024 + (t + 1) * 128],
                      Arep_sb[0:16, nb * 512:(nb + 1) * 512],
                      start=True, stop=True,
                      waits=(["ArepE_1"] if (t == 0 and nb == 0) else []),
                      marks=([f"Bb_{t}"] if nb == 1 else []))
            dve(lambda v, t=t, psx=psx: v.tensor_tensor(M_big[:, t, :], psx[:, :],
                                                        imask, ALU.mult),
                waits=[f"Bb_{t}"], marks=[f"Mb_{t}"])
            ps_last[pn] = f"Mb_{t}"
        for ic in range(8):
            for nb in range(2):
                par = (2 * ic + nb) % 2
                pn = "m0" if par == 0 else "m1"
                psw = ps_m[par]
                for ct in range(8):
                    pe_mm(psw[:, :], pn, wb2[:, ct, ic * 128:(ic + 1) * 128],
                          M_big[:, ct, nb * 512:(nb + 1) * 512],
                          start=(ct == 0), stop=(ct == 7),
                          waits=(["Mb_7", "wl_wvT"]
                                 if (ic == 0 and nb == 0 and ct == 0) else []),
                          marks=([f"wv_{ic}_{nb}"] if ct == 7 else []))
                dve(lambda v, ic=ic, nb=nb, psw=psw:
                    v.tensor_copy(WvE[:, ic, nb * 512:(nb + 1) * 512], psw[:, :]),
                    waits=[f"wv_{ic}_{nb}"], marks=[f"wvE_{ic}_{nb}"])
                ps_last[pn] = f"wvE_{ic}_{nb}"
        WV_ALL = "wvE_7_1"

        SBK = TP // 512
        for cp in range(8):
            for sb in range(SBK):
                par = sb % 2
                pn = "m0" if par == 0 else "m1"
                pso = ps_m[par]
                for ic in range(8):
                    w = []
                    if cp == 0 and sb == 0 and ic == 0:
                        w = [WV_ALL, f"trAe_{NT - 1}"]
                    pe_mm(pso[:, :], pn, WvE[:, ic, cp * 128:(cp + 1) * 128],
                          big1[:, ic, sb * 512:(sb + 1) * 512],
                          start=(ic == 0), stop=(ic == 7),
                          waits=w, marks=([f"op_{cp}_{sb}"] if ic == 7 else []))
                ew = [f"op_{cp}_{sb}"]
                if sb == 0 and cp >= 1:
                    ew += [f"shf_{cp - 1}_{i}" for i in range(4)]
                dve(lambda v, cp=cp, sb=sb, pso=pso:
                    v.tensor_copy(opT[:, sb * 512:(sb + 1) * 512], pso[:, :]),
                    waits=ew, marks=[f"opE_{cp}_{sb}"])
                ps_last[pn] = f"opE_{cp}_{sb}"
            shf_waits = [f"opE_{cp}_{SBK - 1}"]
            if cp == 0:
                shf_waits.append(WV_ALL)
            else:
                shf_waits.append(f"woMM_{cp - 1}_{QCP - 1}_1")
            si = 0
            for hl in range(2):
                for pi in range(2):
                    src = opT.rearrange("p (jh two u) -> p jh two u",
                                        jh=8, two=2)[hl * 64:hl * 64 + 64, :, pi, :]
                    dst = slab[pi * 64:pi * 64 + 64, :].rearrange(
                        "p (t q u) -> p t q u", t=8, q=2)[:, :, hl, :]
                    P.dma("scalarq", dst, src, "s_shf",
                          waits=(shf_waits if (hl == 0 and pi == 0) else []),
                          marks=[f"shf_{cp}_{si}"])
                    si += 1
            SHF = [f"shf_{cp}_{i}" for i in range(4)]
            for q in range(QCP):
                T = QCP * cp + q
                tag = NT + T
                if T + 3 < NTC and T + 3 >= 4:
                    xload(xout_d, T + 3, waits=[f"y_{T - 1}"] if T >= 1 else [f"xn_{NT - 1}"])
                if T + 2 < NTC and T + 2 >= 4:
                    ln2_stats(T + 2)
                if T + 1 < NTC and T + 1 >= 2:
                    ffn_mm(T + 1)
                if T + 2 < NTC and T + 2 >= 4:
                    ln2_norm_tr(T + 2)
                for ob in range(2):
                    half = ps_b1[:, ob * 512:(ob + 1) * 512]
                    pn = f"b1h{ob}"
                    w = []
                    if ob == 0:
                        w = list(SHF)
                        if cp == 0 and q == 0:
                            w.append("Mb_7")
                    pe_mm(half, pn, ones1[:, :],
                          bias_s[:, 1024 + ob * 512:1024 + (ob + 1) * 512],
                          start=True, stop=False, waits=w)
                    for t8 in range(8):
                        pe_mm(half, pn,
                              slab[:, t8 * 2 * UC + q * 128:t8 * 2 * UC + (q + 1) * 128],
                              wb0[:, t8, ob * 512:(ob + 1) * 512],
                              start=False, stop=(t8 == 7),
                              waits=(["wl_wo"]
                                     if (cp == 0 and q == 0 and ob == 0 and t8 == 0) else []),
                              marks=([f"woMM_{cp}_{q}_{ob}"] if t8 == 7 else []))
                    wse = [f"woMM_{cp}_{q}_{ob}"] + ([f"p_{T - 2}"] if T >= 2 else [])
                    act(lambda s, T=T, ob=ob: s.activation(
                        s_tile[T % 2][:, ob * 512:(ob + 1) * 512],
                        ps_b1[:, ob * 512:(ob + 1) * 512], AF.Copy),
                        waits=wse, marks=[f"sevac_{T}_{ob}"])
                    ps_last[pn] = f"sevac_{T}_{ob}"
                dve(lambda v, T=T: v.tensor_tensor(p_tile[T % 2], s_tile[T % 2],
                                                   g_pool[:, T % 2, :], ALU.mult),
                    waits=[f"sevac_{T}_0", f"sevac_{T}_1", f"g_{T}_0", f"g_{T}_1"],
                    marks=[f"p_{T}"])
                wy = [f"p_{T}"]
                if T >= 2:
                    wy.append(f"yst_{T - 2}")
                dve(lambda v, T=T, tag=tag: v.tensor_tensor(
                    y_buf[T % 2], p_tile[T % 2], x_raw[:, tag % 4, :], ALU.add),
                    waits=wy, marks=[f"y_{T}"])
                P.dma("gpsimd", y_d[T * 128:(T + 1) * 128, :], y_buf[T % 2],
                      "s_yst", waits=[f"y_{T}"], marks=[f"yst_{T}"])

        P.op("gpsimd", lambda s: None,
             waits=[f"yst_{T}" for T in range(NTC - 4, NTC)])

        P.resolve()
        block = stk.enter_context(nc.Block())

        def make_body(name):
            def body(eng):
                for s in P.steps:
                    if ENGMAP[s["stream"]] != name:
                        continue
                    for w in s["waits"]:
                        if isinstance(w, tuple) and w[0] == "val":
                            if w[2] > 0:
                                eng.wait_ge(sems[w[1]], w[2])
                        else:
                            semn, v = P.markv[w]
                            if v > 0:
                                eng.wait_ge(sems[semn], v)
                    sname = CHAINED.get(s["stream"])
                    if sname is not None and s["selfwait"] > 0:
                        eng.wait_ge(sems[sname], s["selfwait"])
                    inst = s["fn"](eng)
                    if inst is None:
                        continue
                    for semn, n in s["incs"]:
                        if n:
                            inst.then_inc(sems[semn], n)
                        else:
                            inst.then_inc(sems[semn])
            return body

        block.tensor(make_body("tensor"))
        block.vector(make_body("vector"))
        block.scalar(make_body("scalar"))
        block.gpsimd(make_body("gpsimd"))
        block.sync(make_body("sync"))

    return nc



def _consts():
    idrep = np.zeros((16, 1024), BF)
    for a in range(16):
        idrep[a, a * 64:(a + 1) * 64] = 1
    e2 = np.zeros((16, 1024), BF)
    for t in range(8):
        for beta in range(2):
            e2[2 * t + beta, t * 128 + beta * 64:t * 128 + (beta + 1) * 64] = 1
    cAv = np.concatenate([idrep, e2], axis=1)
    im = np.zeros((128, 1024), BF)
    for p in range(128):
        for h in range(16):
            im[p, h * 64 + (p % 64)] = 1
    eye = np.eye(128, dtype=BF)
    eye16 = np.eye(16, dtype=np.float32)
    sel2 = np.zeros((128, 2), np.float32)
    sel2[0:64, 0] = 1
    sel2[64:128, 1] = 1
    return cAv, im, eye, eye16, sel2


def kernel(L0_attn_x, L0_attn_Wq, L0_attn_Wk, L0_attn_Wv, L0_attn_Wo, L0_attn_bo,
           L0_ffn_W, L0_ffn_b, L0_ffn_lora_down, L0_ffn_lora_up,
           L0_ffn_down_aux, L0_ffn_up_aux):
    x = np.ascontiguousarray(np.asarray(L0_attn_x, np.float32))
    N, S, _ = x.shape
    TP = S // 4
    UC = TP // 16
    U = S // 16
    cAv, im, eye, eye16, sel2 = _consts()
    bias2 = np.zeros((1, 2048), BF)
    bias2[0, 0:1024] = np.asarray(L0_ffn_b, np.float32).astype(BF)
    bias2[0, 1024:2048] = np.asarray(L0_attn_bo, np.float32).astype(BF)
    common = dict(
        wq=np.ascontiguousarray(np.asarray(L0_attn_Wq, np.float32)),
        wk=np.ascontiguousarray(np.asarray(L0_attn_Wk, np.float32)),
        wvT=np.ascontiguousarray(np.asarray(L0_attn_Wv, np.float32).T),
        wo=np.ascontiguousarray(np.asarray(L0_attn_Wo, np.float32)),
        wT=np.ascontiguousarray(np.asarray(L0_ffn_W, np.float32).T),
        luT=np.ascontiguousarray(np.asarray(L0_ffn_lora_up, np.float32).T),
        ld=np.ascontiguousarray(np.asarray(L0_ffn_lora_down, np.float32)),
        uaT=np.ascontiguousarray(np.asarray(L0_ffn_up_aux, np.float32).T),
        da=np.ascontiguousarray(np.asarray(L0_ffn_down_aux, np.float32)),
        cie=cAv, bias2=bias2, imask=im, eye=eye, eye16=eye16, sel2=sel2,
    )
    in_maps = []
    for core in range(NC):
        n, c = divmod(core, 4)
        xt = np.ascontiguousarray(
            x[n, c * TP:(c + 1) * TP].reshape(UC, 16, D).transpose(1, 0, 2)
            .reshape(TP, D))
        xo = np.ascontiguousarray(
            x[n].reshape(16, U, D)[:, c * UC:(c + 1) * UC, :].reshape(TP, D))
        in_maps.append(dict(common, xtok=xt, xout=xo))
    nc_prog = build_program(TP)
    res = run_bass_kernel_spmd(nc_prog, in_maps, core_ids=list(range(NC)))
    out = np.empty_like(x)
    for core in range(NC):
        n, c = divmod(core, 4)
        y = np.asarray(res.results[core]["y"], np.float32)
        out[n].reshape(16, U, D)[:, c * UC:(c + 1) * UC, :] = y.reshape(16, UC, D)
    return out


# version 12
# speedup vs baseline: 1.1131x; 1.1131x over previous
import numpy as np
import ml_dtypes
import concourse.bass as bass
import concourse.mybir as mybir
from concourse.bass_utils import run_bass_kernel_spmd

F32 = mybir.dt.float32
BF16 = mybir.dt.bfloat16
AF = mybir.ActivationFunctionType
ALU = mybir.AluOpType
AX = mybir.AxisListType
BF = ml_dtypes.bfloat16

D = 1024
NC = 8
CHAINED = {"vector": "s_dv", "scalar": "s_ac", "gpsimd_c": "s_gc"}
ENGMAP = {"vector": "vector", "scalar": "scalar", "gpsimd_c": "gpsimd",
          "tensor": "tensor", "sync": "sync", "gpsimd": "gpsimd",
          "scalarq": "scalar"}


class Prog:

    def __init__(self):
        self.steps = []
        self._dmacnt = {}
        self._rr = {}

    def op(self, stream, fn, waits=(), incs=(), marks=()):
        self.steps.append(dict(stream=stream, fn=fn, waits=list(waits),
                               incs=list(incs), marks=list(marks), selfwait=0))

    def dma(self, stream, out, in_, sem, waits=(), marks=(), lanes=4):
        lane = self._rr.get(sem, 0)
        self._rr[sem] = (lane + 1) % lanes
        s = f"{sem}{lane}"
        cur = self._dmacnt.get(s, 0)
        self._dmacnt[s] = cur + 16
        w = list(waits) + ([("val", s, cur)] if cur else [])
        self.op(stream, lambda e, o=out, i=in_: e.dma_start(out=o, in_=i),
                waits=w, incs=[(s, 16)], marks=marks)

    def resolve(self):
        cnt = {}
        self.markv = {}
        for s in self.steps:
            sname = CHAINED.get(s["stream"])
            if sname is not None:
                s["selfwait"] = cnt.get(sname, 0)
                s["incs"] = s["incs"] + [(sname, 1)]
            for semn, n in s["incs"]:
                cnt[semn] = cnt.get(semn, 0) + (n if n else 1)
            for mk in s["marks"]:
                semn = s["incs"][-1][0] if s["incs"] else None
                assert semn is not None, f"mark {mk} on step without inc"
                self.markv[mk] = (semn, cnt[semn])

    def emit(self, engines, sems):
        for s in self.steps:
            eng = engines[ENGMAP[s["stream"]]]
            for w in s["waits"]:
                if isinstance(w, tuple) and w[0] == "val":
                    if w[2] > 0:
                        eng.wait_ge(sems[w[1]], w[2])
                else:
                    semn, v = self.markv[w]
                    if v > 0:
                        eng.wait_ge(sems[semn], v)
            sname = CHAINED.get(s["stream"])
            if sname is not None and s["selfwait"] > 0:
                eng.wait_ge(sems[sname], s["selfwait"])
            inst = s["fn"](eng)
            for semn, n in s["incs"]:
                if n:
                    inst.then_inc(sems[semn], n)
                else:
                    inst.then_inc(sems[semn])


def build_program(TP=4096):
    NT = TP // 128
    UC = TP // 16
    QCP = (2 * UC) // 128
    NTC = NT
    assert QCP * 8 == NTC and UC % 128 == 0

    nc = bass.Bass("TRN2", target_bir_lowering=False, debug=False)

    xtok_d = nc.dram_tensor("xtok", [TP, D], F32, kind="ExternalInput").ap()
    xout_d = nc.dram_tensor("xout", [TP, D], F32, kind="ExternalInput").ap()
    wq_d = nc.dram_tensor("wq", [D, D], F32, kind="ExternalInput").ap()
    wk_d = nc.dram_tensor("wk", [D, D], F32, kind="ExternalInput").ap()
    wvT_d = nc.dram_tensor("wvT", [D, D], F32, kind="ExternalInput").ap()
    wo_d = nc.dram_tensor("wo", [D, D], F32, kind="ExternalInput").ap()
    wT_d = nc.dram_tensor("wT", [D, D], F32, kind="ExternalInput").ap()
    luT_d = nc.dram_tensor("luT", [4, 50], F32, kind="ExternalInput").ap()
    ld_d = nc.dram_tensor("ld", [4, 100], F32, kind="ExternalInput").ap()
    uaT_d = nc.dram_tensor("uaT", [50, D], F32, kind="ExternalInput").ap()
    da_d = nc.dram_tensor("da", [100, D], F32, kind="ExternalInput").ap()
    cie_d = nc.dram_tensor("cie", [16, 2048], BF16, kind="ExternalInput").ap()
    bias2_d = nc.dram_tensor("bias2", [1, 2048], BF16, kind="ExternalInput").ap()
    imask_d = nc.dram_tensor("imask", [128, D], BF16, kind="ExternalInput").ap()
    eye_d = nc.dram_tensor("eye", [128, 128], BF16, kind="ExternalInput").ap()
    eye16_d = nc.dram_tensor("eye16", [16, 16], F32, kind="ExternalInput").ap()
    sel2_d = nc.dram_tensor("sel2", [128, 2], F32, kind="ExternalInput").ap()
    y_d = nc.dram_tensor("y", [TP, D], F32, kind="ExternalOutput").ap()
    cc_in = nc.dram_tensor("cc_in", [16, 16], F32)
    cc_out = nc.dram_tensor("cc_out", [16, 16], F32)

    gidx = {}
    for ii in range(8):
        for jj in range(ii, 8):
            gidx[(ii, jj)] = len(gidx)

    from contextlib import ExitStack
    with ExitStack() as stk:
        e = stk.enter_context
        big0 = e(nc.sbuf_tensor("big0", [128, max(NT, 32) * D], BF16))
        big1 = e(nc.sbuf_tensor("big1", [128, 8, TP], BF16))
        wb0 = e(nc.sbuf_tensor("wb0", [128, 8, D], BF16))
        wb1 = e(nc.sbuf_tensor("wb1", [128, 8, D], BF16))
        wb2 = e(nc.sbuf_tensor("wb2", [128, 8, D], BF16))
        Gp = e(nc.sbuf_tensor("Gp", [128, 36, 128], BF16))
        x_raw = e(nc.sbuf_tensor("x_raw", [128, 4, D], F32))
        eye = e(nc.sbuf_tensor("eye_s", [128, 128], BF16))
        eye16 = e(nc.sbuf_tensor("eye16_s", [16, 16], F32))
        sel2 = e(nc.sbuf_tensor("sel2_s", [128, 2], F32))
        ones1 = e(nc.sbuf_tensor("ones1", [1, 128], BF16))
        lu_s = e(nc.sbuf_tensor("lu_s", [4, 50], BF16))
        ld_s = e(nc.sbuf_tensor("ld_s", [4, 100], BF16))
        st6 = e(nc.sbuf_tensor("st6", [128, 12], F32))
        st2 = e(nc.sbuf_tensor("st2", [128, 4], F32))
        nmean = e(nc.sbuf_tensor("nmean", [128, 2], F32))
        pwin = e(nc.sbuf_tensor("pwin", [128, 2], F32))
        pwex = e(nc.sbuf_tensor("pwex", [128, 1], F32))
        rstd_g = e(nc.sbuf_tensor("rstd_g", [128, 2], F32))
        ET_sb = e(nc.sbuf_tensor("ET_sb", [16, 16], F32))
        ETs_sb = e(nc.sbuf_tensor("ETs_sb", [16, 16], F32))
        E_sb = e(nc.sbuf_tensor("E_sb", [16, 16], F32))
        A_sb = e(nc.sbuf_tensor("A_sb", [16, 16], F32))
        Ab_sb = e(nc.sbuf_tensor("Ab_sb", [16, 16], BF16))
        sm1 = e(nc.sbuf_tensor("sm1", [16, 1], F32))
        sm2 = e(nc.sbuf_tensor("sm2", [16, 1], F32))
        R_sb = e(nc.sbuf_tensor("R_sb", [128, 16], F32))

        ps_b0 = e(nc.psum_tensor("ps_b0", [128, D], F32))
        ps_b1 = e(nc.psum_tensor("ps_b1", [128, D], F32))
        ps_t0 = e(nc.psum_tensor("ps_t0", [128, 8, 128], BF16))
        ps_t1 = e(nc.psum_tensor("ps_t1", [128, 8, 128], BF16))
        ps_m0 = e(nc.psum_tensor("ps_m0", [128, 512], F32))
        ps_m1 = e(nc.psum_tensor("ps_m1", [128, 512], F32))
        ps_b = [ps_b0, ps_b1]
        ps_t = [ps_t0, ps_t1]
        ps_m = [ps_m0, ps_m1]

        dma_bases = ["s_xl", "s_yst", "s_ccb", "s_wl", "s_shf"]
        sem_names = (["s_dv", "s_ac", "s_pe", "s_gc", "s_cc"]
                     + [f"{b}{i}" for b in dma_bases for i in range(4)])
        sems = {n: e(nc.semaphore(n)) for n in sem_names}

        def B0(lo, n):
            return big0[:, lo:lo + n]
        xn_tok = lambda k: B0(k * D, D)
        H_sb = B0(0, 8 * D).rearrange("p (c n) -> p c n", c=8)
        Fm = B0(8 * D, 2 * D).bitcast(F32)
        mscr = B0(10 * D, 2 * 7 * 128).rearrange("p (b c n) -> p b c n", b=2, c=7)
        Arep_sb = B0(0, D)
        xn2_t = [B0(D, D), B0(22 * D, D)]
        g_pool = B0(2 * D, 2 * D).rearrange("p (s n) -> p s n", s=2)
        y_buf = [B0(4 * D, 2 * D).bitcast(F32), B0(6 * D, 2 * D).bitcast(F32)]
        cie = B0(13 * D, 2 * D)
        imask = B0(12 * D, D)
        slab = B0(8 * D, 16 * UC)
        opT = B0(12 * D, TP)
        s_tile = [B0(16 * D, D), B0(17 * D, D)]
        p_tile = [B0(18 * D, D), B0(19 * D, D)]
        slabT = B0(20 * D, 2 * 8 * 128).rearrange("p (b c n) -> p b c n", b=2, c=8)
        M_big = B0(4 * D, 8 * D).rearrange("p (c n) -> p c n", c=8)
        WvE = B0(24 * D, 8 * D).rearrange("p (c n) -> p c n", c=8)

        P = Prog()
        ps_last = {}

        def pe_mm(ps_ap, psname, lhsT, rhs, start, stop, marks=(), waits=()):
            w = list(waits)
            if start and ps_last.get(psname):
                w.append(ps_last[psname])
                ps_last[psname] = None
            P.op("tensor", lambda t, o=ps_ap, l=lhsT, r=rhs, s1=start, s2=stop:
                 t.matmul(o, l, r, start=s1, stop=s2),
                 waits=w, incs=([("s_pe", 1)] if marks else []), marks=marks)

        def pe_tr(ps_ap, psname, in_, ident, start_bank, marks=(), waits=()):
            w = list(waits)
            if start_bank and ps_last.get(psname):
                w.append(ps_last[psname])
                ps_last[psname] = None
            P.op("tensor", lambda t, o=ps_ap, i=in_, d=ident: t.transpose(o, i, d),
                 waits=w, incs=([("s_pe", 1)] if marks else []), marks=marks)

        def dve(fn, waits=(), marks=()):
            P.op("vector", fn, waits=waits, marks=marks)

        def act(fn, waits=(), marks=()):
            P.op("scalar", fn, waits=waits, marks=marks)

        def gdve(fn, waits=(), marks=()):
            P.op("gpsimd_c", fn, waits=waits, marks=marks)

        def gload(out, in_, name, waits=()):
            P.dma("gpsimd", out, in_, "s_wl", waits=waits, marks=[f"wl_{name}"])
        gload(eye[:, :], eye_d[:, :], "eye")
        gload(eye16[:, :], eye16_d[:, :], "eye16")
        gload(sel2[:, :], sel2_d[:, :], "sel2")
        gload(lu_s[:, :], luT_d[:, :], "lu")
        gload(ld_s[:, :], ld_d[:, :], "ld")
        bias_s = Gp[0:1, 0:16, 0:128].rearrange("p a b -> p (a b)")
        ua_s = Gp[0:50, 0:8, 0:128].rearrange("p a b -> p (a b)")
        da_s = Gp[0:100, 8:16, 0:128].rearrange("p a b -> p (a b)")
        r1_s = Gp[0:100, 16:24, 0:128].rearrange("p a b -> p (a b)")
        s1_s = Gp[0:50, 24, 0:100]
        gload(ua_s, uaT_d[:, :], "ua")
        gload(da_s, da_d[:, :], "da")
        P.op("gpsimd_c", lambda g: g.memset(ones1[:, :], 1.0))
        P.op("gpsimd_c", lambda g: g.memset(pwex[:, :], -0.5), marks=["ms_pw"])
        gload(wb1[:, :, :], wT_d.rearrange("(c p) n -> p c n", p=128), "wT")
        gload(wb0[:, :, :], wk_d.rearrange("(c p) n -> p c n", p=128), "wk")
        gload(wb2[:, :, :], wq_d.rearrange("(c p) n -> p c n", p=128), "wq")

        def ln_stats(src, tag):
            par = tag % 2
            dve(lambda v, src=src: v.bn_stats(st6[:, 0:6], src[:, 0:512]),
                waits=[f"xld_{tag}"])
            dve(lambda v, src=src: v.bn_stats(st6[:, 6:12], src[:, 512:1024]))
            w = [f"xn_{tag - 2}"] if tag >= 2 else []
            dve(lambda v, par=par: v.bn_aggr(st2[:, 2 * par:2 * par + 2], st6[:, :]),
                waits=w, marks=[f"sqin_{tag}"])
            gw = [f"sqin_{tag}"]
            if tag == 0:
                gw.append("ms_pw")
            gdve(lambda g, par=par: g.tensor_scalar(pwin[:, par:par + 1],
                                                    st2[:, 2 * par + 1:2 * par + 2],
                                                    1e-5, None, ALU.add),
                 waits=gw)
            gdve(lambda g, par=par: g.tensor_tensor(rstd_g[:, par:par + 1],
                                                    pwin[:, par:par + 1],
                                                    pwex[:, :], ALU.pow),
                 marks=[f"pw_{tag}"])

        def ln_norm(dst, src, tag, war_wait=None):
            par = tag % 2
            wn = [f"pw_{tag}"] + ([war_wait] if war_wait else [])
            dve(lambda v, dst=dst, src=src, par=par: v.tensor_scalar(
                dst, src, st2[:, 2 * par:2 * par + 1], rstd_g[:, par:par + 1],
                ALU.subtract, ALU.mult),
                waits=wn, marks=[f"xn_{tag}"])

        xli = 0
        def xload(dram, k, waits=()):
            nonlocal xli
            P.dma("sync", x_raw[:, xli % 4, :], dram[k * 128:(k + 1) * 128, :],
                  "s_xl", waits=waits, marks=[f"xld_{xli}"])
            xli += 1
        for k in range(4):
            xload(xtok_d, k)
        for k in range(4, NT):
            xload(xtok_d, k, waits=[f"xn_{k - 4}"])
        xload(xout_d, 0, waits=[f"xn_{NT - 4}"])
        xload(xout_d, 1, waits=[f"xn_{NT - 3}"])
        xload(xout_d, 2, waits=[f"xn_{NT - 2}"])
        xload(xout_d, 3, waits=[f"xn_{NT - 1}"])

        def li_step(j):
            if j == 0:
                pe_mm(ps_m0[0:50, 0:100], "m0", lu_s[:, :], ld_s[:, :], True, True,
                      marks=["li_s1"],
                      waits=["wl_lu", "wl_ld", "wl_ua", "wl_da", ("val", "s_gc", 1)])
            elif j == 1:
                dve(lambda v: v.tensor_copy(s1_s, ps_m0[0:50, 0:100]),
                    waits=["li_s1"], marks=["li_s1e"])
                ps_last["m0"] = "li_s1e"
            elif j in (2, 3):
                nb = j - 2
                pn = "m0" if nb == 0 else "m1"
                pe_mm(ps_m[nb][0:100, :], pn, s1_s,
                      ua_s[:, nb * 512:(nb + 1) * 512], True, True,
                      marks=[f"li_r1_{nb}"], waits=(["li_s1e"] if nb == 0 else []))
            elif j in (4, 5):
                nb = j - 4
                dve(lambda v, nb=nb: v.tensor_copy(r1_s[:, nb * 512:(nb + 1) * 512],
                                                   ps_m[nb][0:100, :]),
                    waits=[f"li_r1_{nb}"], marks=[f"li_r1e_{nb}"])
                ps_last["m0" if nb == 0 else "m1"] = f"li_r1e_{nb}"
            elif 6 <= j < 22:
                ic, nb = divmod(j - 6, 2)
                par = (2 * ic + nb) % 2
                pn = "m0" if par == 0 else "m1"
                psx = ps_m[par]
                pe_mm(psx[:, :], pn, da_s[:, ic * 128:(ic + 1) * 128],
                      r1_s[:, nb * 512:(nb + 1) * 512], True, True,
                      marks=[f"lilo_{ic}_{nb}"],
                      waits=(["li_r1e_1", "wl_wT"] if (ic == 0 and nb == 0) else []))
                dve(lambda v, ic=ic, nb=nb, psx=psx:
                    v.tensor_tensor(wb1[:, ic, nb * 512:(nb + 1) * 512],
                                    wb1[:, ic, nb * 512:(nb + 1) * 512],
                                    psx[:, :], ALU.add),
                    waits=[f"lilo_{ic}_{nb}"], marks=[f"weff_{ic}_{nb}"])
                ps_last[pn] = f"weff_{ic}_{nb}"
        WEFF_DONE = "weff_7_1"

        def emit_tr(k):
            pn = "t0" if k % 2 == 0 else "t1"
            for c in range(8):
                pe_tr(ps_t[k % 2][:, c, :], pn,
                      xn_tok(k)[:, c * 128:(c + 1) * 128], eye[:, :],
                      start_bank=(c == 0),
                      waits=([f"xn_{k}", "wl_eye"] if c == 0 else []),
                      marks=([f"trA_{k}"] if c == 7 else []))
            act(lambda s, k=k: s.activation(big1[:, :, k * 128:(k + 1) * 128],
                                            ps_t[k % 2][:, :, :], AF.Copy),
                waits=[f"trA_{k}"], marks=[f"trAe_{k}"])
            ps_last[pn] = f"trAe_{k}"

        def g_chunks(i):
            w_total = D - 128 * i
            chunks = []
            off = 0
            while off < w_total:
                chunks.append((off, min(512, w_total - off)))
                off += min(512, w_total - off)
            return chunks

        def emit_G_s(i, s, wait_xn):
            par = i % 2
            pn = "b0" if par == 0 else "b1"
            psg = ps_b[par]
            chunks = g_chunks(i)
            for ci, (off, w) in enumerate(chunks):
                wt = [f"xn_{s}"] if (ci == 0 and wait_xn) else []
                pe_mm(psg[:, off:off + w], pn,
                      xn_tok(s)[:, i * 128:(i + 1) * 128],
                      xn_tok(s)[:, 128 * i + off:128 * i + off + w],
                      start=(s == 0), stop=(s == NT - 1),
                      waits=wt,
                      marks=([f"G_{i}"] if (s == NT - 1 and ci == len(chunks) - 1) else []))

        def emit_G_evac(i):
            par = i % 2
            pn = "b0" if par == 0 else "b1"
            psg = ps_b[par]
            for jj in range(i, 8):
                act(lambda s, i=i, jj=jj, psg=psg:
                    s.activation(Gp[:, gidx[(i, jj)], :],
                                 psg[:, 128 * (jj - i):128 * (jj - i + 1)], AF.Copy),
                    waits=([f"G_{i}"] if jj == i else []),
                    marks=([f"Ge_{i}"] if jj == 7 else []))
            ps_last[pn] = f"Ge_{i}"

        for k in range(NT):
            ln_stats(x_raw[:, k % 4, :], k)
            if k < 22:
                li_step(k)
            if k >= 1:
                ln_norm(xn_tok(k - 1), x_raw[:, (k - 1) % 4, :], k - 1)
                emit_tr(k - 1)
                emit_G_s(0, k - 1, wait_xn=True)
                emit_G_s(1, k - 1, wait_xn=True)
        for j in range(min(NT, 22), 22):
            li_step(j)
        ln_norm(xn_tok(NT - 1), x_raw[:, (NT - 1) % 4, :], NT - 1)
        emit_tr(NT - 1)
        emit_G_s(0, NT - 1, wait_xn=True)
        emit_G_s(1, NT - 1, wait_xn=True)
        emit_G_evac(0)
        emit_G_evac(1)
        for i in range(2, 8):
            for s in range(NT):
                emit_G_s(i, s, wait_xn=(s == 0 and i == 2))
            emit_G_evac(i)

        gload(imask, imask_d[:, :], "imask", waits=[f"G_{7}"])

        def mirror_tr(i):
            par = i % 2
            pn = "t0" if par == 0 else "t1"
            nblk = 7 - i
            if nblk == 0:
                return
            for bi, J in enumerate(range(i + 1, 8)):
                pe_tr(ps_t[par][:, bi, :], pn, Gp[:, gidx[(i, J)], :], eye[:, :],
                      start_bank=(bi == 0),
                      waits=([f"Ge_{i}"] if bi == 0 else []),
                      marks=([f"GmT_{i}"] if bi == nblk - 1 else []))
            act(lambda s, i=i, par=par, nblk=nblk:
                s.activation(mscr[:, par, 0:nblk, :], ps_t[par][:, 0:nblk, :], AF.Copy),
                waits=[f"GmT_{i}"], marks=[f"GmE_{i}"])
            ps_last[pn] = f"GmE_{i}"
        mirror_tr(0)
        for i in range(8):
            if i + 1 < 8:
                mirror_tr(i + 1)
            for nb in range(2):
                par = (2 * i + nb) % 2
                pn = "b0" if par == 0 else "b1"
                psh = ps_b[par]
                for J in range(8):
                    lhsT = (Gp[:, gidx[(J, i)], :] if J <= i
                            else mscr[:, i % 2, J - i - 1, :])
                    wt = []
                    if nb == 0 and J == 0:
                        wt = ([f"GmE_{i}", "wl_wk"] if i < 7 else ["wl_wk"])
                    pe_mm(psh[:, 0:512], pn, lhsT,
                          wb0[:, J, nb * 512:(nb + 1) * 512],
                          start=(J == 0), stop=(J == 7),
                          waits=wt, marks=([f"H_{i}_{nb}"] if J == 7 else []))
                act(lambda s, i=i, nb=nb, psh=psh:
                    s.activation(H_sb[:, i, nb * 512:(nb + 1) * 512], psh[:, 0:512],
                                 AF.Copy),
                    waits=[f"H_{i}_{nb}"], marks=[f"He_{i}_{nb}"])
                ps_last[pn] = f"He_{i}_{nb}"
        H_ALL = "H_7_1"
        gload(wb0[:, :, :], wo_d.rearrange("(c p) n -> p c n", p=128), "wo",
              waits=[H_ALL])
        gload(bias_s, bias2_d[:, :], "bias2")

        for al in range(8):
            par = al % 2
            pn = "b0" if par == 0 else "b1"
            psf = ps_b[par]
            for i in range(8):
                for fb in range(2):
                    pe_mm(psf[:, fb * 512:(fb + 1) * 512], pn,
                          wb2[:, i, al * 128:(al + 1) * 128],
                          H_sb[:, i, fb * 512:(fb + 1) * 512],
                          start=(i == 0), stop=(i == 7),
                          waits=(["He_7_1", "wl_wq"]
                                 if (al == 0 and i == 0 and fb == 0) else []),
                          marks=([f"F_{al}"] if (i == 7 and fb == 1) else []))
            dve(lambda v, psf=psf: v.tensor_tensor(Fm, psf[:, :], imask, ALU.mult),
                waits=([f"F_{al}", "wl_imask"] if al == 0 else [f"F_{al}"]))
            dve(lambda v: v.tensor_reduce(R_sb[:, :],
                                          Fm.rearrange("p (a b) -> p a b", a=16),
                                          AX.X, ALU.add),
                marks=[f"R_{al}"])
            ps_last[pn] = f"R_{al}"
            pe_mm(ps_m0[0:16, 2 * al:2 * al + 2], "m0", R_sb[:, :], sel2[:, :],
                  start=(al == 0), stop=(al == 7),
                  waits=([f"R_{al}", "wl_sel2"] if al == 0 else [f"R_{al}"]),
                  marks=[f"ETmm_{al}"])
        dve(lambda v: v.tensor_copy(ET_sb[:, :], ps_m0[0:16, 0:16]),
            waits=["ETmm_7"], marks=["ET_evac"])
        ps_last["m0"] = "ET_evac"

        P.dma("sync", cc_in[:, :], ET_sb[:, :], "s_ccb", waits=["ET_evac"],
              marks=["ccb_in"])
        P.op("gpsimd", lambda g: g.collective_compute(
            "AllReduce", ALU.add, replica_groups=[[0, 1, 2, 3], [4, 5, 6, 7]],
            ins=[cc_in.ap().opt()], outs=[cc_out.ap().opt()]),
            waits=["ccb_in"], incs=[("s_cc", None)])
        P.dma("sync", ETs_sb[:, :], cc_out[:, :], "s_ccb",
              waits=[("val", "s_cc", 1)], marks=["ccb_out"])
        gload(wb2[:, :, :], wvT_d.rearrange("(c p) n -> p c n", p=128), "wvT",
              waits=["ETmm_7"])
        gload(cie[0:16, :], cie_d[:, :], "cie", waits=["ETmm_7"])

        def ln2_stats(T):
            ln_stats(x_raw[:, (NT + T) % 4, :], NT + T)

        def ln2_norm_tr(T):
            tag = NT + T
            war = f"trF_{T - 2}" if T >= 2 else None
            ln_norm(xn2_t[T % 2], x_raw[:, tag % 4, :], tag, war_wait=war)
            parT = T % 2
            pnT = "t0" if parT == 0 else "t1"
            for c in range(8):
                pe_tr(ps_t[parT][:, c, :], pnT, xn2_t[T % 2][:, c * 128:(c + 1) * 128],
                      eye[:, :], start_bank=(c == 0),
                      waits=([f"xn_{tag}"] if c == 0 else []),
                      marks=([f"trF_{T}"] if c == 7 else []))
            ew = [f"trF_{T}"]
            if T >= 2:
                ew.append(f"ffn_{T - 2}_1")
            act(lambda s, parT=parT: s.activation(slabT[:, parT, :, :],
                                                  ps_t[parT][:, :, :], AF.Copy),
                waits=ew, marks=[f"slabTe_{T}"])
            ps_last[pnT] = f"slabTe_{T}"

        def ffn_mm(T):
            parT = T % 2
            for ob in range(2):
                half = ps_b0[:, ob * 512:(ob + 1) * 512]
                pn = f"b0h{ob}"
                pe_mm(half, pn, ones1[:, :],
                      bias_s[:, ob * 512:(ob + 1) * 512],
                      start=True, stop=False,
                      waits=([f"slabTe_{T}", WEFF_DONE, "wl_bias2"] if ob == 0 else []))
                for ic in range(8):
                    pe_mm(half, pn, slabT[:, parT, ic, :],
                          wb1[:, ic, ob * 512:(ob + 1) * 512],
                          start=False, stop=(ic == 7),
                          marks=([f"ffn_{T}_{ob}"] if ic == 7 else []))
                wsil = [f"ffn_{T}_{ob}"]
                if T >= 2:
                    wsil.append(f"p_{T - 2}")
                act(lambda s, T=T, ob=ob: s.activation(
                    g_pool[:, T % 2, ob * 512:(ob + 1) * 512],
                    ps_b0[:, ob * 512:(ob + 1) * 512], AF.Silu),
                    waits=wsil, marks=[f"g_{T}_{ob}"])
                ps_last[pn] = f"g_{T}_{ob}"

        ln2_stats(0)
        ln2_norm_tr(0)
        ln2_stats(1)
        ln2_norm_tr(1)
        ffn_mm(0)
        ln2_stats(2)
        ln2_norm_tr(2)
        ffn_mm(1)
        ln2_stats(3)
        ln2_norm_tr(3)

        pe_tr(ps_m1[0:16, 0:16], "m1", ETs_sb[:, :], eye16[:, :], start_bank=True,
              waits=["ccb_out", "wl_eye16"], marks=["Etr"])
        dve(lambda v: v.tensor_copy(E_sb[:, :], ps_m1[0:16, 0:16]), waits=["Etr"],
            marks=["Etr_e"])
        ps_last["m1"] = "Etr_e"
        dve(lambda v: v.reduce_max(sm1[:, :], E_sb[:, :], AX.X))
        dve(lambda v: v.tensor_scalar(sm2[:, :], sm1[:, :], -1.0, None, ALU.mult))
        dve(lambda v: v.tensor_scalar(E_sb[:, :], E_sb[:, :], sm2[:, :], 0.125,
                                      ALU.add, ALU.mult), marks=["smx_in"])
        act(lambda s: s.activation(A_sb[:, :], E_sb[:, :], AF.Exp),
            waits=["smx_in"], marks=["smx_out"])
        dve(lambda v: v.reduce_sum(sm1[:, :], A_sb[:, :], AX.X), waits=["smx_out"])
        dve(lambda v: v.reciprocal(sm2[:, :], sm1[:, :]))
        dve(lambda v: v.tensor_scalar(A_sb[:, :], A_sb[:, :], sm2[:, :], None, ALU.mult))
        dve(lambda v: v.tensor_copy(Ab_sb[:, :], A_sb[:, :]), marks=["A_bf"])
        for nb in range(2):
            pe_mm(ps_m0[0:16, 0:512], "m0", Ab_sb[:, :],
                  cie[0:16, nb * 512:(nb + 1) * 512],
                  start=True, stop=True,
                  waits=(["A_bf", "wl_cie"] if nb == 0 else []),
                  marks=[f"Arep_{nb}"])
            dve(lambda v, nb=nb: v.tensor_copy(Arep_sb[0:16, nb * 512:(nb + 1) * 512],
                                               ps_m0[0:16, 0:512]),
                waits=([f"Arep_{nb}"] + (["He_7_1"] if nb == 0 else [])),
                marks=[f"ArepE_{nb}"])
            ps_last["m0"] = f"ArepE_{nb}"
        for t in range(8):
            par = t % 2
            pn = "b0" if par == 0 else "b1"
            psx = ps_b[par]
            for nb in range(2):
                pe_mm(psx[:, nb * 512:(nb + 1) * 512], pn,
                      cie[0:16, 1024 + t * 128:1024 + (t + 1) * 128],
                      Arep_sb[0:16, nb * 512:(nb + 1) * 512],
                      start=True, stop=True,
                      waits=(["ArepE_1"] if (t == 0 and nb == 0) else []),
                      marks=([f"Bb_{t}"] if nb == 1 else []))
            dve(lambda v, t=t, psx=psx: v.tensor_tensor(M_big[:, t, :], psx[:, :],
                                                        imask, ALU.mult),
                waits=[f"Bb_{t}"], marks=[f"Mb_{t}"])
            ps_last[pn] = f"Mb_{t}"
        for ic in range(8):
            for nb in range(2):
                par = (2 * ic + nb) % 2
                pn = "m0" if par == 0 else "m1"
                psw = ps_m[par]
                for ct in range(8):
                    pe_mm(psw[:, :], pn, wb2[:, ct, ic * 128:(ic + 1) * 128],
                          M_big[:, ct, nb * 512:(nb + 1) * 512],
                          start=(ct == 0), stop=(ct == 7),
                          waits=(["Mb_7", "wl_wvT"]
                                 if (ic == 0 and nb == 0 and ct == 0) else []),
                          marks=([f"wv_{ic}_{nb}"] if ct == 7 else []))
                dve(lambda v, ic=ic, nb=nb, psw=psw:
                    v.tensor_copy(WvE[:, ic, nb * 512:(nb + 1) * 512], psw[:, :]),
                    waits=[f"wv_{ic}_{nb}"], marks=[f"wvE_{ic}_{nb}"])
                ps_last[pn] = f"wvE_{ic}_{nb}"
        WV_ALL = "wvE_7_1"

        SBK = TP // 512
        for cp in range(8):
            for sb in range(SBK):
                par = sb % 2
                pn = "m0" if par == 0 else "m1"
                pso = ps_m[par]
                for ic in range(8):
                    w = []
                    if cp == 0 and sb == 0 and ic == 0:
                        w = [WV_ALL, f"trAe_{NT - 1}"]
                    pe_mm(pso[:, :], pn, WvE[:, ic, cp * 128:(cp + 1) * 128],
                          big1[:, ic, sb * 512:(sb + 1) * 512],
                          start=(ic == 0), stop=(ic == 7),
                          waits=w, marks=([f"op_{cp}_{sb}"] if ic == 7 else []))
                ew = [f"op_{cp}_{sb}"]
                if sb == 0 and cp >= 1:
                    ew += [f"shf_{cp - 1}_{i}" for i in range(4)]
                dve(lambda v, cp=cp, sb=sb, pso=pso:
                    v.tensor_copy(opT[:, sb * 512:(sb + 1) * 512], pso[:, :]),
                    waits=ew, marks=[f"opE_{cp}_{sb}"])
                ps_last[pn] = f"opE_{cp}_{sb}"
            shf_waits = [f"opE_{cp}_{SBK - 1}"]
            if cp == 0:
                shf_waits.append(WV_ALL)
            else:
                shf_waits.append(f"woMM_{cp - 1}_{QCP - 1}_1")
            si = 0
            for hl in range(2):
                for pi in range(2):
                    src = opT.rearrange("p (jh two u) -> p jh two u",
                                        jh=8, two=2)[hl * 64:hl * 64 + 64, :, pi, :]
                    dst = slab[pi * 64:pi * 64 + 64, :].rearrange(
                        "p (t q u) -> p t q u", t=8, q=2)[:, :, hl, :]
                    P.dma("scalarq", dst, src, "s_shf",
                          waits=(shf_waits if (hl == 0 and pi == 0) else []),
                          marks=[f"shf_{cp}_{si}"])
                    si += 1
            SHF = [f"shf_{cp}_{i}" for i in range(4)]
            for q in range(QCP):
                T = QCP * cp + q
                tag = NT + T
                if T + 3 < NTC and T + 3 >= 4:
                    xload(xout_d, T + 3, waits=[f"y_{T - 1}"] if T >= 1 else [f"xn_{NT - 1}"])
                if T + 2 < NTC and T + 2 >= 4:
                    ln2_stats(T + 2)
                if T + 1 < NTC and T + 1 >= 2:
                    ffn_mm(T + 1)
                if T + 2 < NTC and T + 2 >= 4:
                    ln2_norm_tr(T + 2)
                for ob in range(2):
                    half = ps_b1[:, ob * 512:(ob + 1) * 512]
                    pn = f"b1h{ob}"
                    w = []
                    if ob == 0:
                        w = list(SHF)
                        if cp == 0 and q == 0:
                            w.append("Mb_7")
                    pe_mm(half, pn, ones1[:, :],
                          bias_s[:, 1024 + ob * 512:1024 + (ob + 1) * 512],
                          start=True, stop=False, waits=w)
                    for t8 in range(8):
                        pe_mm(half, pn,
                              slab[:, t8 * 2 * UC + q * 128:t8 * 2 * UC + (q + 1) * 128],
                              wb0[:, t8, ob * 512:(ob + 1) * 512],
                              start=False, stop=(t8 == 7),
                              waits=(["wl_wo"]
                                     if (cp == 0 and q == 0 and ob == 0 and t8 == 0) else []),
                              marks=([f"woMM_{cp}_{q}_{ob}"] if t8 == 7 else []))
                    wse = [f"woMM_{cp}_{q}_{ob}"] + ([f"p_{T - 2}"] if T >= 2 else [])
                    act(lambda s, T=T, ob=ob: s.activation(
                        s_tile[T % 2][:, ob * 512:(ob + 1) * 512],
                        ps_b1[:, ob * 512:(ob + 1) * 512], AF.Copy),
                        waits=wse, marks=[f"sevac_{T}_{ob}"])
                    ps_last[pn] = f"sevac_{T}_{ob}"
                dve(lambda v, T=T: v.tensor_tensor(p_tile[T % 2], s_tile[T % 2],
                                                   g_pool[:, T % 2, :], ALU.mult),
                    waits=[f"sevac_{T}_0", f"sevac_{T}_1", f"g_{T}_0", f"g_{T}_1"],
                    marks=[f"p_{T}"])
                wy = [f"p_{T}"]
                if T >= 2:
                    wy.append(f"yst_{T - 2}")
                dve(lambda v, T=T, tag=tag: v.tensor_tensor(
                    y_buf[T % 2], p_tile[T % 2], x_raw[:, tag % 4, :], ALU.add),
                    waits=wy, marks=[f"y_{T}"])
                P.dma("gpsimd", y_d[T * 128:(T + 1) * 128, :], y_buf[T % 2],
                      "s_yst", waits=[f"y_{T}"], marks=[f"yst_{T}"])

        P.op("gpsimd", lambda s: None,
             waits=[f"yst_{T}" for T in range(NTC - 4, NTC)])

        P.resolve()
        block = stk.enter_context(nc.Block())

        def make_body(name):
            def body(eng):
                for s in P.steps:
                    if ENGMAP[s["stream"]] != name:
                        continue
                    for w in s["waits"]:
                        if isinstance(w, tuple) and w[0] == "val":
                            if w[2] > 0:
                                eng.wait_ge(sems[w[1]], w[2])
                        else:
                            semn, v = P.markv[w]
                            if v > 0:
                                eng.wait_ge(sems[semn], v)
                    sname = CHAINED.get(s["stream"])
                    if sname is not None and s["selfwait"] > 0:
                        eng.wait_ge(sems[sname], s["selfwait"])
                    inst = s["fn"](eng)
                    if inst is None:
                        continue
                    for semn, n in s["incs"]:
                        if n:
                            inst.then_inc(sems[semn], n)
                        else:
                            inst.then_inc(sems[semn])
            return body

        block.tensor(make_body("tensor"))
        block.vector(make_body("vector"))
        block.scalar(make_body("scalar"))
        block.gpsimd(make_body("gpsimd"))
        block.sync(make_body("sync"))

    return nc



def _consts():
    idrep = np.zeros((16, 1024), BF)
    for a in range(16):
        idrep[a, a * 64:(a + 1) * 64] = 1
    e2 = np.zeros((16, 1024), BF)
    for t in range(8):
        for beta in range(2):
            e2[2 * t + beta, t * 128 + beta * 64:t * 128 + (beta + 1) * 64] = 1
    cAv = np.concatenate([idrep, e2], axis=1)
    im = np.zeros((128, 1024), BF)
    for p in range(128):
        for h in range(16):
            im[p, h * 64 + (p % 64)] = 1
    eye = np.eye(128, dtype=BF)
    eye16 = np.eye(16, dtype=np.float32)
    sel2 = np.zeros((128, 2), np.float32)
    sel2[0:64, 0] = 1
    sel2[64:128, 1] = 1
    return cAv, im, eye, eye16, sel2


def kernel(L0_attn_x, L0_attn_Wq, L0_attn_Wk, L0_attn_Wv, L0_attn_Wo, L0_attn_bo,
           L0_ffn_W, L0_ffn_b, L0_ffn_lora_down, L0_ffn_lora_up,
           L0_ffn_down_aux, L0_ffn_up_aux):
    x = np.ascontiguousarray(np.asarray(L0_attn_x, np.float32))
    N, S, _ = x.shape
    TP = S // 4
    UC = TP // 16
    U = S // 16
    cAv, im, eye, eye16, sel2 = _consts()
    bias2 = np.zeros((1, 2048), BF)
    bias2[0, 0:1024] = np.asarray(L0_ffn_b, np.float32).astype(BF)
    bias2[0, 1024:2048] = np.asarray(L0_attn_bo, np.float32).astype(BF)
    common = dict(
        wq=np.ascontiguousarray(np.asarray(L0_attn_Wq, np.float32)),
        wk=np.ascontiguousarray(np.asarray(L0_attn_Wk, np.float32)),
        wvT=np.ascontiguousarray(np.asarray(L0_attn_Wv, np.float32).T),
        wo=np.ascontiguousarray(np.asarray(L0_attn_Wo, np.float32)),
        wT=np.ascontiguousarray(np.asarray(L0_ffn_W, np.float32).T),
        luT=np.ascontiguousarray(np.asarray(L0_ffn_lora_up, np.float32).T),
        ld=np.ascontiguousarray(np.asarray(L0_ffn_lora_down, np.float32)),
        uaT=np.ascontiguousarray(np.asarray(L0_ffn_up_aux, np.float32).T),
        da=np.ascontiguousarray(np.asarray(L0_ffn_down_aux, np.float32)),
        cie=cAv, bias2=bias2, imask=im, eye=eye, eye16=eye16, sel2=sel2,
    )
    in_maps = []
    for core in range(NC):
        n, c = divmod(core, 4)
        xt = np.ascontiguousarray(
            x[n, c * TP:(c + 1) * TP].reshape(UC, 16, D).transpose(1, 0, 2)
            .reshape(TP, D))
        xo = np.ascontiguousarray(
            x[n].reshape(16, U, D)[:, c * UC:(c + 1) * UC, :].reshape(TP, D))
        in_maps.append(dict(common, xtok=xt, xout=xo))
    nc_prog = build_program(TP)
    res = run_bass_kernel_spmd(nc_prog, in_maps, core_ids=list(range(NC)))
    out = np.empty_like(x)
    for core in range(NC):
        n, c = divmod(core, 4)
        y = np.asarray(res.results[core]["y"], np.float32)
        out[n].reshape(16, U, D)[:, c * UC:(c + 1) * UC, :] = y.reshape(16, UC, D)
    return out


# version 13 (final)
# speedup vs baseline: 1.1934x; 1.0722x over previous
import numpy as np
import ml_dtypes
import concourse.bass as bass
import concourse.mybir as mybir
from concourse.bass_utils import run_bass_kernel_spmd

F32 = mybir.dt.float32
BF16 = mybir.dt.bfloat16
AF = mybir.ActivationFunctionType
ALU = mybir.AluOpType
AX = mybir.AxisListType
BF = ml_dtypes.bfloat16

D = 1024
NC = 8
CHAINED = {"vector": "s_dv", "scalar": "s_ac", "gpsimd_c": "s_gc"}
ENGMAP = {"vector": "vector", "scalar": "scalar", "gpsimd_c": "gpsimd",
          "tensor": "tensor", "sync": "sync", "gpsimd": "gpsimd",
          "scalarq": "scalar"}


class Prog:

    def __init__(self):
        self.steps = []
        self._dmacnt = {}
        self._rr = {}

    def op(self, stream, fn, waits=(), incs=(), marks=()):
        self.steps.append(dict(stream=stream, fn=fn, waits=list(waits),
                               incs=list(incs), marks=list(marks), selfwait=0))

    def dma(self, stream, out, in_, sem, waits=(), marks=(), lanes=4):
        lane = self._rr.get(sem, 0)
        self._rr[sem] = (lane + 1) % lanes
        s = f"{sem}{lane}"
        cur = self._dmacnt.get(s, 0)
        self._dmacnt[s] = cur + 16
        w = list(waits) + ([("val", s, cur)] if cur else [])
        self.op(stream, lambda e, o=out, i=in_: e.dma_start(out=o, in_=i),
                waits=w, incs=[(s, 16)], marks=marks)

    def resolve(self):
        cnt = {}
        self.markv = {}
        for s in self.steps:
            sname = CHAINED.get(s["stream"])
            if sname is not None:
                s["selfwait"] = cnt.get(sname, 0)
                s["incs"] = s["incs"] + [(sname, 1)]
            for semn, n in s["incs"]:
                cnt[semn] = cnt.get(semn, 0) + (n if n else 1)
            for mk in s["marks"]:
                semn = s["incs"][-1][0] if s["incs"] else None
                assert semn is not None, f"mark {mk} on step without inc"
                self.markv[mk] = (semn, cnt[semn])

    def emit(self, engines, sems):
        for s in self.steps:
            eng = engines[ENGMAP[s["stream"]]]
            for w in s["waits"]:
                if isinstance(w, tuple) and w[0] == "val":
                    if w[2] > 0:
                        eng.wait_ge(sems[w[1]], w[2])
                else:
                    semn, v = self.markv[w]
                    if v > 0:
                        eng.wait_ge(sems[semn], v)
            sname = CHAINED.get(s["stream"])
            if sname is not None and s["selfwait"] > 0:
                eng.wait_ge(sems[sname], s["selfwait"])
            inst = s["fn"](eng)
            for semn, n in s["incs"]:
                if n:
                    inst.then_inc(sems[semn], n)
                else:
                    inst.then_inc(sems[semn])


def build_program(TP=4096):
    NT = TP // 128
    UC = TP // 16
    QCP = (2 * UC) // 128
    NTC = NT
    assert QCP * 8 == NTC and UC % 128 == 0

    nc = bass.Bass("TRN2", target_bir_lowering=False, debug=False)

    xtok_d = nc.dram_tensor("xtok", [TP, D], F32, kind="ExternalInput").ap()
    xout_d = nc.dram_tensor("xout", [TP, D], F32, kind="ExternalInput").ap()
    wq_d = nc.dram_tensor("wq", [D, D], F32, kind="ExternalInput").ap()
    wk_d = nc.dram_tensor("wk", [D, D], F32, kind="ExternalInput").ap()
    wvT_d = nc.dram_tensor("wvT", [D, D], F32, kind="ExternalInput").ap()
    wo_d = nc.dram_tensor("wo", [D, D], F32, kind="ExternalInput").ap()
    wT_d = nc.dram_tensor("wT", [D, D], F32, kind="ExternalInput").ap()
    luT_d = nc.dram_tensor("luT", [4, 50], F32, kind="ExternalInput").ap()
    ld_d = nc.dram_tensor("ld", [4, 100], F32, kind="ExternalInput").ap()
    uaT_d = nc.dram_tensor("uaT", [50, D], F32, kind="ExternalInput").ap()
    da_d = nc.dram_tensor("da", [100, D], F32, kind="ExternalInput").ap()
    cie_d = nc.dram_tensor("cie", [16, 2048], BF16, kind="ExternalInput").ap()
    bias2_d = nc.dram_tensor("bias2", [1, 2048], BF16, kind="ExternalInput").ap()
    imask_d = nc.dram_tensor("imask", [128, D], BF16, kind="ExternalInput").ap()
    eye_d = nc.dram_tensor("eye", [128, 128], BF16, kind="ExternalInput").ap()
    eye16_d = nc.dram_tensor("eye16", [16, 16], F32, kind="ExternalInput").ap()
    sel2_d = nc.dram_tensor("sel2", [128, 2], F32, kind="ExternalInput").ap()
    y_d = nc.dram_tensor("y", [TP, D], F32, kind="ExternalOutput").ap()
    cc_in = nc.dram_tensor("cc_in", [16, 16], F32)
    cc_out = nc.dram_tensor("cc_out", [16, 16], F32)

    gidx = {}
    for ii in range(8):
        for jj in range(ii, 8):
            gidx[(ii, jj)] = len(gidx)

    from contextlib import ExitStack
    with ExitStack() as stk:
        e = stk.enter_context
        big0 = e(nc.sbuf_tensor("big0", [128, max(NT, 32) * D], BF16))
        big1 = e(nc.sbuf_tensor("big1", [128, 8, TP], BF16))
        wb0 = e(nc.sbuf_tensor("wb0", [128, 8, D], BF16))
        wb1 = e(nc.sbuf_tensor("wb1", [128, 8, D], BF16))
        wb2 = e(nc.sbuf_tensor("wb2", [128, 8, D], BF16))
        Gp = e(nc.sbuf_tensor("Gp", [128, 36, 128], BF16))
        x_raw = e(nc.sbuf_tensor("x_raw", [128, 4, D], F32))
        eye = e(nc.sbuf_tensor("eye_s", [128, 128], BF16))
        eye16 = e(nc.sbuf_tensor("eye16_s", [16, 16], F32))
        sel2 = e(nc.sbuf_tensor("sel2_s", [128, 2], F32))
        ones1 = e(nc.sbuf_tensor("ones1", [1, 128], BF16))
        lu_s = e(nc.sbuf_tensor("lu_s", [4, 50], BF16))
        ld_s = e(nc.sbuf_tensor("ld_s", [4, 100], BF16))
        st6 = e(nc.sbuf_tensor("st6", [128, 12], F32))
        st2 = e(nc.sbuf_tensor("st2", [128, 4], F32))
        nmean = e(nc.sbuf_tensor("nmean", [128, 2], F32))
        pwin = e(nc.sbuf_tensor("pwin", [128, 2], F32))
        pwex = e(nc.sbuf_tensor("pwex", [128, 1], F32))
        rstd_g = e(nc.sbuf_tensor("rstd_g", [128, 2], F32))
        ET_sb = e(nc.sbuf_tensor("ET_sb", [16, 16], F32))
        ETs_sb = e(nc.sbuf_tensor("ETs_sb", [16, 16], F32))
        E_sb = e(nc.sbuf_tensor("E_sb", [16, 16], F32))
        A_sb = e(nc.sbuf_tensor("A_sb", [16, 16], F32))
        Ab_sb = e(nc.sbuf_tensor("Ab_sb", [16, 16], BF16))
        sm1 = e(nc.sbuf_tensor("sm1", [16, 1], F32))
        sm2 = e(nc.sbuf_tensor("sm2", [16, 1], F32))
        R_sb = e(nc.sbuf_tensor("R_sb", [128, 16], F32))

        ps_b0 = e(nc.psum_tensor("ps_b0", [128, D], F32))
        ps_b1 = e(nc.psum_tensor("ps_b1", [128, D], F32))
        ps_t0 = e(nc.psum_tensor("ps_t0", [128, 8, 128], BF16))
        ps_t1 = e(nc.psum_tensor("ps_t1", [128, 8, 128], BF16))
        ps_m0 = e(nc.psum_tensor("ps_m0", [128, 512], F32))
        ps_m1 = e(nc.psum_tensor("ps_m1", [128, 512], F32))
        ps_b = [ps_b0, ps_b1]
        ps_t = [ps_t0, ps_t1]
        ps_m = [ps_m0, ps_m1]

        dma_bases = ["s_xl", "s_yst", "s_ccb", "s_wl", "s_shf"]
        sem_names = (["s_dv", "s_ac", "s_pe", "s_gc", "s_cc"]
                     + [f"{b}{i}" for b in dma_bases for i in range(4)])
        sems = {n: e(nc.semaphore(n)) for n in sem_names}

        def B0(lo, n):
            return big0[:, lo:lo + n]
        xn_tok = lambda k: B0(k * D, D)
        H_sb = B0(0, 8 * D).rearrange("p (c n) -> p c n", c=8)
        Fm = B0(8 * D, 2 * D).bitcast(F32)
        mscr = B0(10 * D, 2 * 7 * 128).rearrange("p (b c n) -> p b c n", b=2, c=7)
        Arep_sb = B0(0, D)
        xn2_t = [B0(D, D), B0(22 * D, D)]
        g_pool = B0(2 * D, 2 * D).rearrange("p (s n) -> p s n", s=2)
        y_buf = [B0(4 * D, 2 * D).bitcast(F32), B0(6 * D, 2 * D).bitcast(F32)]
        cie = B0(13 * D, 2 * D)
        imask = B0(12 * D, D)
        slab = B0(8 * D, 16 * UC)
        opT = B0(12 * D, TP)
        s_tile = [B0(16 * D, D), B0(17 * D, D)]
        p_tile = [B0(18 * D, D), B0(19 * D, D)]
        slabT = B0(20 * D, 2 * 8 * 128).rearrange("p (b c n) -> p b c n", b=2, c=8)
        M_big = B0(4 * D, 8 * D).rearrange("p (c n) -> p c n", c=8)
        WvE = B0(24 * D, 8 * D).rearrange("p (c n) -> p c n", c=8)

        P = Prog()
        ps_last = {}

        def pe_mm(ps_ap, psname, lhsT, rhs, start, stop, marks=(), waits=()):
            w = list(waits)
            if start and ps_last.get(psname):
                w.append(ps_last[psname])
                ps_last[psname] = None
            P.op("tensor", lambda t, o=ps_ap, l=lhsT, r=rhs, s1=start, s2=stop:
                 t.matmul(o, l, r, start=s1, stop=s2),
                 waits=w, incs=([("s_pe", 1)] if marks else []), marks=marks)

        def pe_tr(ps_ap, psname, in_, ident, start_bank, marks=(), waits=()):
            w = list(waits)
            if start_bank and ps_last.get(psname):
                w.append(ps_last[psname])
                ps_last[psname] = None
            P.op("tensor", lambda t, o=ps_ap, i=in_, d=ident: t.transpose(o, i, d),
                 waits=w, incs=([("s_pe", 1)] if marks else []), marks=marks)

        def dve(fn, waits=(), marks=()):
            P.op("vector", fn, waits=waits, marks=marks)

        def act(fn, waits=(), marks=()):
            P.op("scalar", fn, waits=waits, marks=marks)

        def gdve(fn, waits=(), marks=()):
            P.op("gpsimd_c", fn, waits=waits, marks=marks)

        def gload(out, in_, name, waits=()):
            P.dma("gpsimd", out, in_, "s_wl", waits=waits, marks=[f"wl_{name}"])
        gload(eye[:, :], eye_d[:, :], "eye")
        gload(eye16[:, :], eye16_d[:, :], "eye16")
        gload(sel2[:, :], sel2_d[:, :], "sel2")
        gload(lu_s[:, :], luT_d[:, :], "lu")
        gload(ld_s[:, :], ld_d[:, :], "ld")
        bias_s = Gp[0:1, 0:16, 0:128].rearrange("p a b -> p (a b)")
        ua_s = Gp[0:50, 0:8, 0:128].rearrange("p a b -> p (a b)")
        da_s = Gp[0:100, 8:16, 0:128].rearrange("p a b -> p (a b)")
        r1_s = Gp[0:100, 16:24, 0:128].rearrange("p a b -> p (a b)")
        s1_s = Gp[0:50, 24, 0:100]
        gload(ua_s, uaT_d[:, :], "ua")
        gload(da_s, da_d[:, :], "da")
        P.op("gpsimd_c", lambda g: g.memset(ones1[:, :], 1.0))
        P.op("gpsimd_c", lambda g: g.memset(pwex[:, :], -0.5), marks=["ms_pw"])
        gload(wb1[:, :, :], wT_d.rearrange("(c p) n -> p c n", p=128), "wT")
        gload(wb0[:, :, :], wk_d.rearrange("(c p) n -> p c n", p=128), "wk")
        gload(wb2[:, :, :], wq_d.rearrange("(c p) n -> p c n", p=128), "wq")

        def ln_stats(src, tag):
            par = tag % 2
            dve(lambda v, src=src: v.bn_stats(st6[:, 0:6], src[:, 0:512]),
                waits=[f"xld_{tag}"])
            dve(lambda v, src=src: v.bn_stats(st6[:, 6:12], src[:, 512:1024]))
            w = [f"xn_{tag - 2}"] if tag >= 2 else []
            dve(lambda v, par=par: v.bn_aggr(st2[:, 2 * par:2 * par + 2], st6[:, :]),
                waits=w, marks=[f"sqin_{tag}"])
            gw = [f"sqin_{tag}"]
            if tag == 0:
                gw.append("ms_pw")
            gdve(lambda g, par=par: g.tensor_scalar(pwin[:, par:par + 1],
                                                    st2[:, 2 * par + 1:2 * par + 2],
                                                    1e-5, None, ALU.add),
                 waits=gw)
            gdve(lambda g, par=par: g.tensor_tensor(rstd_g[:, par:par + 1],
                                                    pwin[:, par:par + 1],
                                                    pwex[:, :], ALU.pow),
                 marks=[f"pw_{tag}"])

        def ln_norm(dst, src, tag, war_wait=None):
            par = tag % 2
            wn = [f"pw_{tag}"] + ([war_wait] if war_wait else [])
            dve(lambda v, dst=dst, src=src, par=par: v.tensor_scalar(
                dst, src, st2[:, 2 * par:2 * par + 1], rstd_g[:, par:par + 1],
                ALU.subtract, ALU.mult),
                waits=wn, marks=[f"xn_{tag}"])

        xli = 0
        def xload(dram, k, waits=()):
            nonlocal xli
            P.dma("sync", x_raw[:, xli % 4, :], dram[k * 128:(k + 1) * 128, :],
                  "s_xl", waits=waits, marks=[f"xld_{xli}"])
            xli += 1
        for k in range(4):
            xload(xtok_d, k)
        for k in range(4, NT):
            xload(xtok_d, k, waits=[f"xn_{k - 4}"])
        xload(xout_d, 0, waits=[f"xn_{NT - 4}"])
        xload(xout_d, 1, waits=[f"xn_{NT - 3}"])
        xload(xout_d, 2, waits=[f"xn_{NT - 2}"])
        xload(xout_d, 3, waits=[f"xn_{NT - 1}"])

        def li_step(j):
            if j == 0:
                pe_mm(ps_m0[0:50, 0:100], "m0", lu_s[:, :], ld_s[:, :], True, True,
                      marks=["li_s1"],
                      waits=["wl_lu", "wl_ld", "wl_ua", "wl_da", ("val", "s_gc", 1)])
            elif j == 1:
                dve(lambda v: v.tensor_copy(s1_s, ps_m0[0:50, 0:100]),
                    waits=["li_s1"], marks=["li_s1e"])
                ps_last["m0"] = "li_s1e"
            elif j in (2, 3):
                nb = j - 2
                pn = "m0" if nb == 0 else "m1"
                pe_mm(ps_m[nb][0:100, :], pn, s1_s,
                      ua_s[:, nb * 512:(nb + 1) * 512], True, True,
                      marks=[f"li_r1_{nb}"], waits=(["li_s1e"] if nb == 0 else []))
            elif j in (4, 5):
                nb = j - 4
                dve(lambda v, nb=nb: v.tensor_copy(r1_s[:, nb * 512:(nb + 1) * 512],
                                                   ps_m[nb][0:100, :]),
                    waits=[f"li_r1_{nb}"], marks=[f"li_r1e_{nb}"])
                ps_last["m0" if nb == 0 else "m1"] = f"li_r1e_{nb}"
            elif 6 <= j < 22:
                ic, nb = divmod(j - 6, 2)
                par = (2 * ic + nb) % 2
                pn = "m0" if par == 0 else "m1"
                psx = ps_m[par]
                pe_mm(psx[:, :], pn, da_s[:, ic * 128:(ic + 1) * 128],
                      r1_s[:, nb * 512:(nb + 1) * 512], True, True,
                      marks=[f"lilo_{ic}_{nb}"],
                      waits=(["li_r1e_1", "wl_wT"] if (ic == 0 and nb == 0) else []))
                dve(lambda v, ic=ic, nb=nb, psx=psx:
                    v.tensor_tensor(wb1[:, ic, nb * 512:(nb + 1) * 512],
                                    wb1[:, ic, nb * 512:(nb + 1) * 512],
                                    psx[:, :], ALU.add),
                    waits=[f"lilo_{ic}_{nb}"], marks=[f"weff_{ic}_{nb}"])
                ps_last[pn] = f"weff_{ic}_{nb}"
        WEFF_DONE = "weff_7_1"

        def emit_tr(k):
            pn = "t0" if k % 2 == 0 else "t1"
            for c in range(8):
                pe_tr(ps_t[k % 2][:, c, :], pn,
                      xn_tok(k)[:, c * 128:(c + 1) * 128], eye[:, :],
                      start_bank=(c == 0),
                      waits=([f"xn_{k}", "wl_eye"] if c == 0 else []),
                      marks=([f"trA_{k}"] if c == 7 else []))
            act(lambda s, k=k: s.activation(big1[:, :, k * 128:(k + 1) * 128],
                                            ps_t[k % 2][:, :, :], AF.Copy),
                waits=[f"trA_{k}"], marks=[f"trAe_{k}"])
            ps_last[pn] = f"trAe_{k}"

        def g_chunks(i):
            w_total = D - 128 * i
            chunks = []
            off = 0
            while off < w_total:
                chunks.append((off, min(512, w_total - off)))
                off += min(512, w_total - off)
            return chunks

        def emit_G_s(i, s, wait_xn):
            par = i % 2
            pn = "b0" if par == 0 else "b1"
            psg = ps_b[par]
            chunks = g_chunks(i)
            for ci, (off, w) in enumerate(chunks):
                wt = [f"xn_{s}"] if (ci == 0 and wait_xn) else []
                pe_mm(psg[:, off:off + w], pn,
                      xn_tok(s)[:, i * 128:(i + 1) * 128],
                      xn_tok(s)[:, 128 * i + off:128 * i + off + w],
                      start=(s == 0), stop=(s == NT - 1),
                      waits=wt,
                      marks=([f"G_{i}"] if (s == NT - 1 and ci == len(chunks) - 1) else []))

        def emit_G_evac(i):
            par = i % 2
            pn = "b0" if par == 0 else "b1"
            psg = ps_b[par]
            for jj in range(i, 8):
                act(lambda s, i=i, jj=jj, psg=psg:
                    s.activation(Gp[:, gidx[(i, jj)], :],
                                 psg[:, 128 * (jj - i):128 * (jj - i + 1)], AF.Copy),
                    waits=([f"G_{i}"] if jj == i else []),
                    marks=([f"Ge_{i}"] if jj == 7 else []))
            ps_last[pn] = f"Ge_{i}"

        for k in range(NT):
            ln_stats(x_raw[:, k % 4, :], k)
            if k < 22:
                li_step(k)
            if k >= 1:
                ln_norm(xn_tok(k - 1), x_raw[:, (k - 1) % 4, :], k - 1)
                emit_tr(k - 1)
                emit_G_s(0, k - 1, wait_xn=True)
                emit_G_s(1, k - 1, wait_xn=True)
        for j in range(min(NT, 22), 22):
            li_step(j)
        ln_norm(xn_tok(NT - 1), x_raw[:, (NT - 1) % 4, :], NT - 1)
        emit_tr(NT - 1)
        emit_G_s(0, NT - 1, wait_xn=True)
        emit_G_s(1, NT - 1, wait_xn=True)
        emit_G_evac(0)
        emit_G_evac(1)
        for i in range(2, 8):
            for s in range(NT):
                emit_G_s(i, s, wait_xn=(s == 0 and i == 2))
            emit_G_evac(i)

        gload(imask, imask_d[:, :], "imask", waits=[f"G_{7}"])

        def mirror_tr(i):
            par = i % 2
            pn = "t0" if par == 0 else "t1"
            nblk = 7 - i
            if nblk == 0:
                return
            for bi, J in enumerate(range(i + 1, 8)):
                pe_tr(ps_t[par][:, bi, :], pn, Gp[:, gidx[(i, J)], :], eye[:, :],
                      start_bank=(bi == 0),
                      waits=([f"Ge_{i}"] if bi == 0 else []),
                      marks=([f"GmT_{i}"] if bi == nblk - 1 else []))
            act(lambda s, i=i, par=par, nblk=nblk:
                s.activation(mscr[:, par, 0:nblk, :], ps_t[par][:, 0:nblk, :], AF.Copy),
                waits=[f"GmT_{i}"], marks=[f"GmE_{i}"])
            ps_last[pn] = f"GmE_{i}"
        mirror_tr(0)
        for i in range(8):
            if i + 1 < 8:
                mirror_tr(i + 1)
            for nb in range(2):
                par = (2 * i + nb) % 2
                pn = "b0" if par == 0 else "b1"
                psh = ps_b[par]
                for J in range(8):
                    lhsT = (Gp[:, gidx[(J, i)], :] if J <= i
                            else mscr[:, i % 2, J - i - 1, :])
                    wt = []
                    if nb == 0 and J == 0:
                        wt = ([f"GmE_{i}", "wl_wk"] if i < 7 else ["wl_wk"])
                    pe_mm(psh[:, 0:512], pn, lhsT,
                          wb0[:, J, nb * 512:(nb + 1) * 512],
                          start=(J == 0), stop=(J == 7),
                          waits=wt, marks=([f"H_{i}_{nb}"] if J == 7 else []))
                act(lambda s, i=i, nb=nb, psh=psh:
                    s.activation(H_sb[:, i, nb * 512:(nb + 1) * 512], psh[:, 0:512],
                                 AF.Copy),
                    waits=[f"H_{i}_{nb}"], marks=[f"He_{i}_{nb}"])
                ps_last[pn] = f"He_{i}_{nb}"
        H_ALL = "H_7_1"
        gload(wb0[:, :, :], wo_d.rearrange("(c p) n -> p c n", p=128), "wo",
              waits=[H_ALL])
        gload(bias_s, bias2_d[:, :], "bias2")

        for al in range(8):
            par = al % 2
            pn = "b0" if par == 0 else "b1"
            psf = ps_b[par]
            for i in range(8):
                for fb in range(2):
                    pe_mm(psf[:, fb * 512:(fb + 1) * 512], pn,
                          wb2[:, i, al * 128:(al + 1) * 128],
                          H_sb[:, i, fb * 512:(fb + 1) * 512],
                          start=(i == 0), stop=(i == 7),
                          waits=(["He_7_1", "wl_wq"]
                                 if (al == 0 and i == 0 and fb == 0) else []),
                          marks=([f"F_{al}"] if (i == 7 and fb == 1) else []))
            dve(lambda v, psf=psf: v.tensor_tensor(Fm, psf[:, :], imask, ALU.mult),
                waits=([f"F_{al}", "wl_imask"] if al == 0 else [f"F_{al}"]))
            dve(lambda v: v.tensor_reduce(R_sb[:, :],
                                          Fm.rearrange("p (a b) -> p a b", a=16),
                                          AX.X, ALU.add),
                marks=[f"R_{al}"])
            ps_last[pn] = f"R_{al}"
            pe_mm(ps_m0[0:16, 2 * al:2 * al + 2], "m0", R_sb[:, :], sel2[:, :],
                  start=(al == 0), stop=(al == 7),
                  waits=([f"R_{al}", "wl_sel2"] if al == 0 else [f"R_{al}"]),
                  marks=[f"ETmm_{al}"])
        dve(lambda v: v.tensor_copy(ET_sb[:, :], ps_m0[0:16, 0:16]),
            waits=["ETmm_7"], marks=["ET_evac"])
        ps_last["m0"] = "ET_evac"

        P.dma("sync", cc_in[:, :], ET_sb[:, :], "s_ccb", waits=["ET_evac"],
              marks=["ccb_in"])
        P.op("gpsimd", lambda g: g.collective_compute(
            "AllReduce", ALU.add, replica_groups=[[0, 1, 2, 3], [4, 5, 6, 7]],
            ins=[cc_in.ap().opt()], outs=[cc_out.ap().opt()]),
            waits=["ccb_in"], incs=[("s_cc", None)])
        P.dma("sync", ETs_sb[:, :], cc_out[:, :], "s_ccb",
              waits=[("val", "s_cc", 1)], marks=["ccb_out"])
        gload(wb2[:, :, :], wvT_d.rearrange("(c p) n -> p c n", p=128), "wvT",
              waits=["ETmm_7"])
        gload(cie[0:16, :], cie_d[:, :], "cie", waits=["ETmm_7"])

        def ln2_stats(T):
            ln_stats(x_raw[:, (NT + T) % 4, :], NT + T)

        def ln2_norm_tr(T):
            tag = NT + T
            war = f"trF_{T - 2}" if T >= 2 else None
            ln_norm(xn2_t[T % 2], x_raw[:, tag % 4, :], tag, war_wait=war)
            parT = T % 2
            pnT = "t0" if parT == 0 else "t1"
            for c in range(8):
                pe_tr(ps_t[parT][:, c, :], pnT, xn2_t[T % 2][:, c * 128:(c + 1) * 128],
                      eye[:, :], start_bank=(c == 0),
                      waits=([f"xn_{tag}"] if c == 0 else []),
                      marks=([f"trF_{T}"] if c == 7 else []))
            ew = [f"trF_{T}"]
            if T >= 2:
                ew.append(f"ffn_{T - 2}_1")
            act(lambda s, parT=parT: s.activation(slabT[:, parT, :, :],
                                                  ps_t[parT][:, :, :], AF.Copy),
                waits=ew, marks=[f"slabTe_{T}"])
            ps_last[pnT] = f"slabTe_{T}"

        def ffn_mm(T):
            parT = T % 2
            for ob in range(2):
                half = ps_b0[:, ob * 512:(ob + 1) * 512]
                pn = f"b0h{ob}"
                pe_mm(half, pn, ones1[:, :],
                      bias_s[:, ob * 512:(ob + 1) * 512],
                      start=True, stop=False,
                      waits=([f"slabTe_{T}", WEFF_DONE, "wl_bias2"] if ob == 0 else []))
                for ic in range(8):
                    pe_mm(half, pn, slabT[:, parT, ic, :],
                          wb1[:, ic, ob * 512:(ob + 1) * 512],
                          start=False, stop=(ic == 7),
                          marks=([f"ffn_{T}_{ob}"] if ic == 7 else []))
                wsil = [f"ffn_{T}_{ob}"]
                if T >= 2:
                    wsil.append(f"p_{T - 2}")
                act(lambda s, T=T, ob=ob: s.activation(
                    g_pool[:, T % 2, ob * 512:(ob + 1) * 512],
                    ps_b0[:, ob * 512:(ob + 1) * 512], AF.Silu),
                    waits=wsil, marks=[f"g_{T}_{ob}"])
                ps_last[pn] = f"g_{T}_{ob}"

        ln2_stats(0)
        ln2_norm_tr(0)
        ln2_stats(1)
        ln2_norm_tr(1)
        ffn_mm(0)
        ln2_stats(2)
        ln2_norm_tr(2)
        ffn_mm(1)
        ln2_stats(3)
        ln2_norm_tr(3)

        pe_tr(ps_m1[0:16, 0:16], "m1", ETs_sb[:, :], eye16[:, :], start_bank=True,
              waits=["ccb_out", "wl_eye16"], marks=["Etr"])
        dve(lambda v: v.tensor_copy(E_sb[:, :], ps_m1[0:16, 0:16]), waits=["Etr"],
            marks=["Etr_e"])
        ps_last["m1"] = "Etr_e"
        dve(lambda v: v.reduce_max(sm1[:, :], E_sb[:, :], AX.X))
        dve(lambda v: v.tensor_scalar(sm2[:, :], sm1[:, :], -1.0, None, ALU.mult))
        dve(lambda v: v.tensor_scalar(E_sb[:, :], E_sb[:, :], sm2[:, :], 0.125,
                                      ALU.add, ALU.mult), marks=["smx_in"])
        act(lambda s: s.activation(A_sb[:, :], E_sb[:, :], AF.Exp),
            waits=["smx_in"], marks=["smx_out"])
        dve(lambda v: v.reduce_sum(sm1[:, :], A_sb[:, :], AX.X), waits=["smx_out"])
        dve(lambda v: v.reciprocal(sm2[:, :], sm1[:, :]))
        dve(lambda v: v.tensor_scalar(A_sb[:, :], A_sb[:, :], sm2[:, :], None, ALU.mult))
        dve(lambda v: v.tensor_copy(Ab_sb[:, :], A_sb[:, :]), marks=["A_bf"])
        for nb in range(2):
            pe_mm(ps_m0[0:16, 0:512], "m0", Ab_sb[:, :],
                  cie[0:16, nb * 512:(nb + 1) * 512],
                  start=True, stop=True,
                  waits=(["A_bf", "wl_cie"] if nb == 0 else []),
                  marks=[f"Arep_{nb}"])
            dve(lambda v, nb=nb: v.tensor_copy(Arep_sb[0:16, nb * 512:(nb + 1) * 512],
                                               ps_m0[0:16, 0:512]),
                waits=([f"Arep_{nb}"] + (["He_7_1"] if nb == 0 else [])),
                marks=[f"ArepE_{nb}"])
            ps_last["m0"] = f"ArepE_{nb}"
        for t in range(8):
            par = t % 2
            pn = "b0" if par == 0 else "b1"
            psx = ps_b[par]
            for nb in range(2):
                pe_mm(psx[:, nb * 512:(nb + 1) * 512], pn,
                      cie[0:16, 1024 + t * 128:1024 + (t + 1) * 128],
                      Arep_sb[0:16, nb * 512:(nb + 1) * 512],
                      start=True, stop=True,
                      waits=(["ArepE_1"] if (t == 0 and nb == 0) else []),
                      marks=([f"Bb_{t}"] if nb == 1 else []))
            dve(lambda v, t=t, psx=psx: v.tensor_tensor(M_big[:, t, :], psx[:, :],
                                                        imask, ALU.mult),
                waits=[f"Bb_{t}"], marks=[f"Mb_{t}"])
            ps_last[pn] = f"Mb_{t}"
        for ic in range(8):
            for nb in range(2):
                par = (2 * ic + nb) % 2
                pn = "m0" if par == 0 else "m1"
                psw = ps_m[par]
                for ct in range(8):
                    pe_mm(psw[:, :], pn, wb2[:, ct, ic * 128:(ic + 1) * 128],
                          M_big[:, ct, nb * 512:(nb + 1) * 512],
                          start=(ct == 0), stop=(ct == 7),
                          waits=(["Mb_7", "wl_wvT"]
                                 if (ic == 0 and nb == 0 and ct == 0) else []),
                          marks=([f"wv_{ic}_{nb}"] if ct == 7 else []))
                dve(lambda v, ic=ic, nb=nb, psw=psw:
                    v.tensor_copy(WvE[:, ic, nb * 512:(nb + 1) * 512], psw[:, :]),
                    waits=[f"wv_{ic}_{nb}"], marks=[f"wvE_{ic}_{nb}"])
                ps_last[pn] = f"wvE_{ic}_{nb}"
        WV_ALL = "wvE_7_1"

        SBK = TP // 512
        for cp in range(8):
            for sb in range(SBK):
                par = sb % 2
                pn = "m0" if par == 0 else "m1"
                pso = ps_m[par]
                for ic in range(8):
                    w = []
                    if cp == 0 and sb == 0 and ic == 0:
                        w = [WV_ALL, f"trAe_{NT - 1}"]
                    pe_mm(pso[:, :], pn, WvE[:, ic, cp * 128:(cp + 1) * 128],
                          big1[:, ic, sb * 512:(sb + 1) * 512],
                          start=(ic == 0), stop=(ic == 7),
                          waits=w, marks=([f"op_{cp}_{sb}"] if ic == 7 else []))
                ew = [f"op_{cp}_{sb}"]
                if sb == 0 and cp >= 1:
                    ew += [f"shf_{cp - 1}_{i}" for i in range(4)]
                dve(lambda v, cp=cp, sb=sb, pso=pso:
                    v.tensor_copy(opT[:, sb * 512:(sb + 1) * 512], pso[:, :]),
                    waits=ew, marks=[f"opE_{cp}_{sb}"])
                ps_last[pn] = f"opE_{cp}_{sb}"
            shf_waits = [f"opE_{cp}_{SBK - 1}"]
            if cp == 0:
                shf_waits.append(WV_ALL)
            else:
                shf_waits.append(f"woMM_{cp - 1}_{QCP - 1}_1")
            si = 0
            for hl in range(2):
                for pi in range(2):
                    src = opT.rearrange("p (jh two u) -> p jh two u",
                                        jh=8, two=2)[hl * 64:hl * 64 + 64, :, pi, :]
                    dst = slab[pi * 64:pi * 64 + 64, :].rearrange(
                        "p (t q u) -> p t q u", t=8, q=2)[:, :, hl, :]
                    P.dma("scalarq", dst, src, "s_shf",
                          waits=(shf_waits if (hl == 0 and pi == 0) else []),
                          marks=[f"shf_{cp}_{si}"])
                    si += 1
            SHF = [f"shf_{cp}_{i}" for i in range(4)]
            for q in range(QCP):
                T = QCP * cp + q
                tag = NT + T
                if T + 3 < NTC and T + 3 >= 4:
                    xload(xout_d, T + 3, waits=[f"y_{T - 1}"] if T >= 1 else [f"xn_{NT - 1}"])
                if T + 2 < NTC and T + 2 >= 4:
                    ln2_stats(T + 2)
                if T >= 1:
                    Tp = T - 1
                    dve(lambda v, Tp=Tp: v.tensor_tensor(p_tile[Tp % 2], s_tile[Tp % 2],
                                                         g_pool[:, Tp % 2, :], ALU.mult),
                        waits=[f"sevac_{Tp}_0", f"sevac_{Tp}_1", f"g_{Tp}_0", f"g_{Tp}_1"],
                        marks=[f"p_{Tp}"])
                    wy = [f"p_{Tp}"]
                    if Tp >= 2:
                        wy.append(f"yst_{Tp - 2}")
                    dve(lambda v, Tp=Tp: v.tensor_tensor(
                        y_buf[Tp % 2], p_tile[Tp % 2], x_raw[:, (NT + Tp) % 4, :],
                        ALU.add),
                        waits=wy, marks=[f"y_{Tp}"])
                    P.dma("gpsimd", y_d[Tp * 128:(Tp + 1) * 128, :], y_buf[Tp % 2],
                          "s_yst", waits=[f"y_{Tp}"], marks=[f"yst_{Tp}"])
                if T + 1 < NTC and T + 1 >= 2:
                    ffn_mm(T + 1)
                if T + 2 < NTC and T + 2 >= 4:
                    ln2_norm_tr(T + 2)
                for ob in range(2):
                    half = ps_b1[:, ob * 512:(ob + 1) * 512]
                    pn = f"b1h{ob}"
                    w = []
                    if ob == 0:
                        w = list(SHF)
                        if cp == 0 and q == 0:
                            w.append("Mb_7")
                    pe_mm(half, pn, ones1[:, :],
                          bias_s[:, 1024 + ob * 512:1024 + (ob + 1) * 512],
                          start=True, stop=False, waits=w)
                    for t8 in range(8):
                        pe_mm(half, pn,
                              slab[:, t8 * 2 * UC + q * 128:t8 * 2 * UC + (q + 1) * 128],
                              wb0[:, t8, ob * 512:(ob + 1) * 512],
                              start=False, stop=(t8 == 7),
                              waits=(["wl_wo"]
                                     if (cp == 0 and q == 0 and ob == 0 and t8 == 0) else []),
                              marks=([f"woMM_{cp}_{q}_{ob}"] if t8 == 7 else []))
                    wse = [f"woMM_{cp}_{q}_{ob}"] + ([f"p_{T - 2}"] if T >= 2 else [])
                    act(lambda s, T=T, ob=ob: s.activation(
                        s_tile[T % 2][:, ob * 512:(ob + 1) * 512],
                        ps_b1[:, ob * 512:(ob + 1) * 512], AF.Copy),
                        waits=wse, marks=[f"sevac_{T}_{ob}"])
                    ps_last[pn] = f"sevac_{T}_{ob}"
        Tp = NTC - 1
        dve(lambda v, Tp=Tp: v.tensor_tensor(p_tile[Tp % 2], s_tile[Tp % 2],
                                             g_pool[:, Tp % 2, :], ALU.mult),
            waits=[f"sevac_{Tp}_0", f"sevac_{Tp}_1", f"g_{Tp}_0", f"g_{Tp}_1"],
            marks=[f"p_{Tp}"])
        dve(lambda v, Tp=Tp: v.tensor_tensor(
            y_buf[Tp % 2], p_tile[Tp % 2], x_raw[:, (NT + Tp) % 4, :], ALU.add),
            waits=[f"p_{Tp}", f"yst_{Tp - 2}"], marks=[f"y_{Tp}"])
        P.dma("gpsimd", y_d[Tp * 128:(Tp + 1) * 128, :], y_buf[Tp % 2],
              "s_yst", waits=[f"y_{Tp}"], marks=[f"yst_{Tp}"])

        P.op("gpsimd", lambda s: None,
             waits=[f"yst_{T}" for T in range(NTC - 4, NTC)])

        P.resolve()
        block = stk.enter_context(nc.Block())

        def make_body(name):
            def body(eng):
                for s in P.steps:
                    if ENGMAP[s["stream"]] != name:
                        continue
                    for w in s["waits"]:
                        if isinstance(w, tuple) and w[0] == "val":
                            if w[2] > 0:
                                eng.wait_ge(sems[w[1]], w[2])
                        else:
                            semn, v = P.markv[w]
                            if v > 0:
                                eng.wait_ge(sems[semn], v)
                    sname = CHAINED.get(s["stream"])
                    if sname is not None and s["selfwait"] > 0:
                        eng.wait_ge(sems[sname], s["selfwait"])
                    inst = s["fn"](eng)
                    if inst is None:
                        continue
                    for semn, n in s["incs"]:
                        if n:
                            inst.then_inc(sems[semn], n)
                        else:
                            inst.then_inc(sems[semn])
            return body

        block.tensor(make_body("tensor"))
        block.vector(make_body("vector"))
        block.scalar(make_body("scalar"))
        block.gpsimd(make_body("gpsimd"))
        block.sync(make_body("sync"))

    return nc



def _consts():
    idrep = np.zeros((16, 1024), BF)
    for a in range(16):
        idrep[a, a * 64:(a + 1) * 64] = 1
    e2 = np.zeros((16, 1024), BF)
    for t in range(8):
        for beta in range(2):
            e2[2 * t + beta, t * 128 + beta * 64:t * 128 + (beta + 1) * 64] = 1
    cAv = np.concatenate([idrep, e2], axis=1)
    im = np.zeros((128, 1024), BF)
    for p in range(128):
        for h in range(16):
            im[p, h * 64 + (p % 64)] = 1
    eye = np.eye(128, dtype=BF)
    eye16 = np.eye(16, dtype=np.float32)
    sel2 = np.zeros((128, 2), np.float32)
    sel2[0:64, 0] = 1
    sel2[64:128, 1] = 1
    return cAv, im, eye, eye16, sel2


def kernel(L0_attn_x, L0_attn_Wq, L0_attn_Wk, L0_attn_Wv, L0_attn_Wo, L0_attn_bo,
           L0_ffn_W, L0_ffn_b, L0_ffn_lora_down, L0_ffn_lora_up,
           L0_ffn_down_aux, L0_ffn_up_aux):
    x = np.ascontiguousarray(np.asarray(L0_attn_x, np.float32))
    N, S, _ = x.shape
    TP = S // 4
    UC = TP // 16
    U = S // 16
    cAv, im, eye, eye16, sel2 = _consts()
    bias2 = np.zeros((1, 2048), BF)
    bias2[0, 0:1024] = np.asarray(L0_ffn_b, np.float32).astype(BF)
    bias2[0, 1024:2048] = np.asarray(L0_attn_bo, np.float32).astype(BF)
    common = dict(
        wq=np.ascontiguousarray(np.asarray(L0_attn_Wq, np.float32)),
        wk=np.ascontiguousarray(np.asarray(L0_attn_Wk, np.float32)),
        wvT=np.ascontiguousarray(np.asarray(L0_attn_Wv, np.float32).T),
        wo=np.ascontiguousarray(np.asarray(L0_attn_Wo, np.float32)),
        wT=np.ascontiguousarray(np.asarray(L0_ffn_W, np.float32).T),
        luT=np.ascontiguousarray(np.asarray(L0_ffn_lora_up, np.float32).T),
        ld=np.ascontiguousarray(np.asarray(L0_ffn_lora_down, np.float32)),
        uaT=np.ascontiguousarray(np.asarray(L0_ffn_up_aux, np.float32).T),
        da=np.ascontiguousarray(np.asarray(L0_ffn_down_aux, np.float32)),
        cie=cAv, bias2=bias2, imask=im, eye=eye, eye16=eye16, sel2=sel2,
    )
    in_maps = []
    for core in range(NC):
        n, c = divmod(core, 4)
        xt = np.ascontiguousarray(
            x[n, c * TP:(c + 1) * TP].reshape(UC, 16, D).transpose(1, 0, 2)
            .reshape(TP, D))
        xo = np.ascontiguousarray(
            x[n].reshape(16, U, D)[:, c * UC:(c + 1) * UC, :].reshape(TP, D))
        in_maps.append(dict(common, xtok=xt, xout=xo))
    nc_prog = build_program(TP)
    res = run_bass_kernel_spmd(nc_prog, in_maps, core_ids=list(range(NC)))
    out = np.empty_like(x)
    for core in range(NC):
        n, c = divmod(core, 4)
        y = np.asarray(res.results[core]["y"], np.float32)
        out[n].reshape(16, U, D)[:, c * UC:(c + 1) * UC, :] = y.reshape(16, UC, D)
    return out


# version 16
# speedup vs baseline: 1.2221x; 1.0240x over previous
import numpy as np
import ml_dtypes
import concourse.bass as bass
import concourse.mybir as mybir
from concourse.bass_utils import run_bass_kernel_spmd

F32 = mybir.dt.float32
BF16 = mybir.dt.bfloat16
AF = mybir.ActivationFunctionType
ALU = mybir.AluOpType
AX = mybir.AxisListType
BF = ml_dtypes.bfloat16

D = 1024
NC = 8
CHAINED = {"vector": "s_dv", "scalar": "s_ac", "gpsimd_c": "s_gc"}
ENGMAP = {"vector": "vector", "scalar": "scalar", "gpsimd_c": "gpsimd",
          "tensor": "tensor", "sync": "sync", "gpsimd": "gpsimd",
          "scalarq": "scalar"}


class Prog:

    def __init__(self):
        self.steps = []
        self._dmacnt = {}
        self._rr = {}

    def op(self, stream, fn, waits=(), incs=(), marks=()):
        self.steps.append(dict(stream=stream, fn=fn, waits=list(waits),
                               incs=list(incs), marks=list(marks), selfwait=0))

    def dma(self, stream, out, in_, sem, waits=(), marks=(), lanes=4):
        lane = self._rr.get(sem, 0)
        self._rr[sem] = (lane + 1) % lanes
        s = f"{sem}{lane}"
        cur = self._dmacnt.get(s, 0)
        self._dmacnt[s] = cur + 16
        w = list(waits) + ([("val", s, cur)] if cur else [])
        self.op(stream, lambda e, o=out, i=in_: e.dma_start(out=o, in_=i),
                waits=w, incs=[(s, 16)], marks=marks)

    def resolve(self):
        cnt = {}
        self.markv = {}
        for s in self.steps:
            sname = CHAINED.get(s["stream"])
            if sname is not None:
                s["selfwait"] = cnt.get(sname, 0)
                s["incs"] = s["incs"] + [(sname, 1)]
            for semn, n in s["incs"]:
                cnt[semn] = cnt.get(semn, 0) + (n if n else 1)
            for mk in s["marks"]:
                semn = s["incs"][-1][0] if s["incs"] else None
                assert semn is not None, f"mark {mk} on step without inc"
                self.markv[mk] = (semn, cnt[semn])

    def emit(self, engines, sems):
        for s in self.steps:
            eng = engines[ENGMAP[s["stream"]]]
            for w in s["waits"]:
                if isinstance(w, tuple) and w[0] == "val":
                    if w[2] > 0:
                        eng.wait_ge(sems[w[1]], w[2])
                else:
                    semn, v = self.markv[w]
                    if v > 0:
                        eng.wait_ge(sems[semn], v)
            sname = CHAINED.get(s["stream"])
            if sname is not None and s["selfwait"] > 0:
                eng.wait_ge(sems[sname], s["selfwait"])
            inst = s["fn"](eng)
            for semn, n in s["incs"]:
                if n:
                    inst.then_inc(sems[semn], n)
                else:
                    inst.then_inc(sems[semn])


def build_program(TP=4096):
    NT = TP // 128
    UC = TP // 16
    QCP = (2 * UC) // 128
    NTC = NT
    assert QCP * 8 == NTC and UC % 128 == 0

    nc = bass.Bass("TRN2", target_bir_lowering=False, debug=False)

    xtok_d = nc.dram_tensor("xtok", [TP, D], F32, kind="ExternalInput").ap()
    xout_d = nc.dram_tensor("xout", [TP, D], F32, kind="ExternalInput").ap()
    wq_d = nc.dram_tensor("wq", [D, D], F32, kind="ExternalInput").ap()
    wk_d = nc.dram_tensor("wk", [D, D], F32, kind="ExternalInput").ap()
    wvT_d = nc.dram_tensor("wvT", [D, D], F32, kind="ExternalInput").ap()
    wo_d = nc.dram_tensor("wo", [D, D], F32, kind="ExternalInput").ap()
    wT_d = nc.dram_tensor("wT", [D, D], F32, kind="ExternalInput").ap()
    luT_d = nc.dram_tensor("luT", [4, 50], F32, kind="ExternalInput").ap()
    ld_d = nc.dram_tensor("ld", [4, 100], F32, kind="ExternalInput").ap()
    uaT_d = nc.dram_tensor("uaT", [50, D], F32, kind="ExternalInput").ap()
    da_d = nc.dram_tensor("da", [100, D], F32, kind="ExternalInput").ap()
    cie_d = nc.dram_tensor("cie", [16, 2048], BF16, kind="ExternalInput").ap()
    bias2_d = nc.dram_tensor("bias2", [1, 2048], BF16, kind="ExternalInput").ap()
    imask_d = nc.dram_tensor("imask", [128, D], BF16, kind="ExternalInput").ap()
    eye_d = nc.dram_tensor("eye", [128, 128], BF16, kind="ExternalInput").ap()
    eye16_d = nc.dram_tensor("eye16", [16, 16], F32, kind="ExternalInput").ap()
    sel2_d = nc.dram_tensor("sel2", [128, 2], F32, kind="ExternalInput").ap()
    y_d = nc.dram_tensor("y", [TP, D], F32, kind="ExternalOutput").ap()
    cc_in = nc.dram_tensor("cc_in", [16, 16], F32)
    cc_out = nc.dram_tensor("cc_out", [16, 16], F32)

    gidx = {}
    for ii in range(8):
        for jj in range(ii, 8):
            gidx[(ii, jj)] = len(gidx)

    from contextlib import ExitStack
    with ExitStack() as stk:
        e = stk.enter_context
        big0 = e(nc.sbuf_tensor("big0", [128, max(NT, 32) * D], BF16))
        big1 = e(nc.sbuf_tensor("big1", [128, 8, TP], BF16))
        wb0 = e(nc.sbuf_tensor("wb0", [128, 8, D], BF16))
        wb1 = e(nc.sbuf_tensor("wb1", [128, 8, D], BF16))
        wb2 = e(nc.sbuf_tensor("wb2", [128, 8, D], BF16))
        Gp = e(nc.sbuf_tensor("Gp", [128, 36, 128], BF16))
        x_raw = e(nc.sbuf_tensor("x_raw", [128, 4, D], F32))
        eye = e(nc.sbuf_tensor("eye_s", [128, 128], BF16))
        eye16 = e(nc.sbuf_tensor("eye16_s", [16, 16], F32))
        sel2 = e(nc.sbuf_tensor("sel2_s", [128, 2], F32))
        ones1 = e(nc.sbuf_tensor("ones1", [1, 128], BF16))
        lu_s = e(nc.sbuf_tensor("lu_s", [4, 50], BF16))
        ld_s = e(nc.sbuf_tensor("ld_s", [4, 100], BF16))
        st6 = e(nc.sbuf_tensor("st6", [128, 12], F32))
        st2 = e(nc.sbuf_tensor("st2", [128, 4], F32))
        nmean = e(nc.sbuf_tensor("nmean", [128, 2], F32))
        pwin = e(nc.sbuf_tensor("pwin", [128, 2], F32))
        pwex = e(nc.sbuf_tensor("pwex", [128, 1], F32))
        rstd_g = e(nc.sbuf_tensor("rstd_g", [128, 2], F32))
        ET_sb = e(nc.sbuf_tensor("ET_sb", [16, 16], F32))
        ETs_sb = e(nc.sbuf_tensor("ETs_sb", [16, 16], F32))
        E_sb = e(nc.sbuf_tensor("E_sb", [16, 16], F32))
        A_sb = e(nc.sbuf_tensor("A_sb", [16, 16], F32))
        Ab_sb = e(nc.sbuf_tensor("Ab_sb", [16, 16], BF16))
        sm1 = e(nc.sbuf_tensor("sm1", [16, 1], F32))
        sm2 = e(nc.sbuf_tensor("sm2", [16, 1], F32))
        R_sb = e(nc.sbuf_tensor("R_sb", [128, 16], F32))

        ps_b0 = e(nc.psum_tensor("ps_b0", [128, D], F32))
        ps_b1 = e(nc.psum_tensor("ps_b1", [128, D], F32))
        ps_t0 = e(nc.psum_tensor("ps_t0", [128, 8, 128], BF16))
        ps_t1 = e(nc.psum_tensor("ps_t1", [128, 8, 128], BF16))
        ps_m0 = e(nc.psum_tensor("ps_m0", [128, 512], F32))
        ps_m1 = e(nc.psum_tensor("ps_m1", [128, 512], F32))
        ps_b = [ps_b0, ps_b1]
        ps_t = [ps_t0, ps_t1]
        ps_m = [ps_m0, ps_m1]

        dma_bases = ["s_xl", "s_yst", "s_ccb", "s_wl", "s_shf"]
        sem_names = (["s_dv", "s_ac", "s_pe", "s_gc", "s_cc"]
                     + [f"{b}{i}" for b in dma_bases for i in range(4)])
        sems = {n: e(nc.semaphore(n)) for n in sem_names}

        def B0(lo, n):
            return big0[:, lo:lo + n]
        xn_tok = lambda k: B0(k * D, D)
        H_sb = B0(0, 8 * D).rearrange("p (c n) -> p c n", c=8)
        Fm = B0(8 * D, 2 * D).bitcast(F32)
        mscr = B0(10 * D, 2 * 7 * 128).rearrange("p (b c n) -> p b c n", b=2, c=7)
        Arep_sb = B0(0, D)
        xn2_t = [B0(D, D), B0(23 * D, D)]
        g_pool = B0(2 * D, 3 * D).rearrange("p (s n) -> p s n", s=3)
        y_buf = [B0(5 * D, 2 * D).bitcast(F32), B0(7 * D, 2 * D).bitcast(F32)]
        cie = B0(14 * D, 2 * D)
        imask = B0(13 * D, D)
        slab = B0(9 * D, 16 * UC)
        opT = B0(13 * D, TP)
        s_tile = [B0(17 * D, D), B0(18 * D, D)]
        p_tile = [B0(19 * D, D), B0(20 * D, D)]
        slabT = B0(21 * D, 2 * 8 * 128).rearrange("p (b c n) -> p b c n", b=2, c=8)
        M_big = B0(5 * D, 8 * D).rearrange("p (c n) -> p c n", c=8)
        WvE = B0(24 * D, 8 * D).rearrange("p (c n) -> p c n", c=8)

        P = Prog()
        ps_last = {}

        def pe_mm(ps_ap, psname, lhsT, rhs, start, stop, marks=(), waits=()):
            w = list(waits)
            if start and ps_last.get(psname):
                w.append(ps_last[psname])
                ps_last[psname] = None
            P.op("tensor", lambda t, o=ps_ap, l=lhsT, r=rhs, s1=start, s2=stop:
                 t.matmul(o, l, r, start=s1, stop=s2),
                 waits=w, incs=([("s_pe", 1)] if marks else []), marks=marks)

        def pe_tr(ps_ap, psname, in_, ident, start_bank, marks=(), waits=()):
            w = list(waits)
            if start_bank and ps_last.get(psname):
                w.append(ps_last[psname])
                ps_last[psname] = None
            P.op("tensor", lambda t, o=ps_ap, i=in_, d=ident: t.transpose(o, i, d),
                 waits=w, incs=([("s_pe", 1)] if marks else []), marks=marks)

        def dve(fn, waits=(), marks=()):
            P.op("vector", fn, waits=waits, marks=marks)

        def act(fn, waits=(), marks=()):
            P.op("scalar", fn, waits=waits, marks=marks)

        def gdve(fn, waits=(), marks=()):
            P.op("gpsimd_c", fn, waits=waits, marks=marks)

        def gload(out, in_, name, waits=()):
            P.dma("gpsimd", out, in_, "s_wl", waits=waits, marks=[f"wl_{name}"])
        gload(lu_s[:, :], luT_d[:, :], "lu")
        gload(ld_s[:, :], ld_d[:, :], "ld")
        gload(eye[:, :], eye_d[:, :], "eye")
        gload(eye16[:, :], eye16_d[:, :], "eye16")
        gload(sel2[:, :], sel2_d[:, :], "sel2")
        bias_s = Gp[0:1, 0:16, 0:128].rearrange("p a b -> p (a b)")
        ua_s = Gp[0:50, 0:8, 0:128].rearrange("p a b -> p (a b)")
        da_s = Gp[0:100, 8:16, 0:128].rearrange("p a b -> p (a b)")
        r1_s = Gp[0:100, 16:24, 0:128].rearrange("p a b -> p (a b)")
        s1_s = Gp[0:50, 24, 0:100]
        gload(ua_s, uaT_d[:, :], "ua")
        gload(da_s, da_d[:, :], "da")
        P.op("gpsimd_c", lambda g: g.memset(ones1[:, :], 1.0))
        P.op("gpsimd_c", lambda g: g.memset(pwex[:, :], -0.5), marks=["ms_pw"])
        gload(wb1[:, :, :], wT_d.rearrange("(c p) n -> p c n", p=128), "wT")
        gload(wb0[:, :, :], wk_d.rearrange("(c p) n -> p c n", p=128), "wk")
        gload(wb2[:, :, :], wq_d.rearrange("(c p) n -> p c n", p=128), "wq")

        def ln_stats(src, tag):
            par = tag % 2
            dve(lambda v, src=src: v.bn_stats(st6[:, 0:6], src[:, 0:512]),
                waits=[f"xld_{tag}"])
            dve(lambda v, src=src: v.bn_stats(st6[:, 6:12], src[:, 512:1024]))
            w = [f"xn_{tag - 2}"] if tag >= 2 else []
            dve(lambda v, par=par: v.bn_aggr(st2[:, 2 * par:2 * par + 2], st6[:, :]),
                waits=w, marks=[f"sqin_{tag}"])
            gw = [f"sqin_{tag}"]
            if tag == 0:
                gw.append("ms_pw")
            gdve(lambda g, par=par: g.tensor_scalar(pwin[:, par:par + 1],
                                                    st2[:, 2 * par + 1:2 * par + 2],
                                                    1e-5, None, ALU.add),
                 waits=gw)
            gdve(lambda g, par=par: g.tensor_tensor(rstd_g[:, par:par + 1],
                                                    pwin[:, par:par + 1],
                                                    pwex[:, :], ALU.pow),
                 marks=[f"pw_{tag}"])

        def ln_norm(dst, src, tag, war_wait=None):
            par = tag % 2
            wn = [f"pw_{tag}"] + ([war_wait] if war_wait else [])
            dve(lambda v, dst=dst, src=src, par=par: v.tensor_scalar(
                dst, src, st2[:, 2 * par:2 * par + 1], rstd_g[:, par:par + 1],
                ALU.subtract, ALU.mult),
                waits=wn, marks=[f"xn_{tag}"])

        xli = 0
        def xload(dram, k, waits=()):
            nonlocal xli
            P.dma("sync", x_raw[:, xli % 4, :], dram[k * 128:(k + 1) * 128, :],
                  "s_xl", waits=waits, marks=[f"xld_{xli}"])
            xli += 1
        for k in range(4):
            xload(xtok_d, k)
        for k in range(4, NT):
            xload(xtok_d, k, waits=[f"xn_{k - 4}"])
        xload(xout_d, 0, waits=[f"xn_{NT - 4}"])
        xload(xout_d, 1, waits=[f"xn_{NT - 3}"])
        xload(xout_d, 2, waits=[f"xn_{NT - 2}"])
        xload(xout_d, 3, waits=[f"xn_{NT - 1}"])

        def li_step(j):
            if j == 0:
                pe_mm(ps_m0[0:50, 0:100], "m0", lu_s[:, :], ld_s[:, :], True, True,
                      marks=["li_s1"],
                      waits=["wl_lu", "wl_ld", "wl_ua", "wl_da", ("val", "s_gc", 1)])
            elif j == 1:
                dve(lambda v: v.tensor_copy(s1_s, ps_m0[0:50, 0:100]),
                    waits=["li_s1"], marks=["li_s1e"])
                ps_last["m0"] = "li_s1e"
            elif j in (2, 3):
                nb = j - 2
                pn = "m0" if nb == 0 else "m1"
                pe_mm(ps_m[nb][0:100, :], pn, s1_s,
                      ua_s[:, nb * 512:(nb + 1) * 512], True, True,
                      marks=[f"li_r1_{nb}"], waits=(["li_s1e"] if nb == 0 else []))
            elif j in (4, 5):
                nb = j - 4
                dve(lambda v, nb=nb: v.tensor_copy(r1_s[:, nb * 512:(nb + 1) * 512],
                                                   ps_m[nb][0:100, :]),
                    waits=[f"li_r1_{nb}"], marks=[f"li_r1e_{nb}"])
                ps_last["m0" if nb == 0 else "m1"] = f"li_r1e_{nb}"
            elif 6 <= j < 22:
                ic, nb = divmod(j - 6, 2)
                par = (2 * ic + nb) % 2
                pn = "m0" if par == 0 else "m1"
                psx = ps_m[par]
                pe_mm(psx[:, :], pn, da_s[:, ic * 128:(ic + 1) * 128],
                      r1_s[:, nb * 512:(nb + 1) * 512], True, True,
                      marks=[f"lilo_{ic}_{nb}"],
                      waits=(["li_r1e_1", "wl_wT"] if (ic == 0 and nb == 0) else []))
                dve(lambda v, ic=ic, nb=nb, psx=psx:
                    v.tensor_tensor(wb1[:, ic, nb * 512:(nb + 1) * 512],
                                    wb1[:, ic, nb * 512:(nb + 1) * 512],
                                    psx[:, :], ALU.add),
                    waits=[f"lilo_{ic}_{nb}"], marks=[f"weff_{ic}_{nb}"])
                ps_last[pn] = f"weff_{ic}_{nb}"
        WEFF_DONE = "weff_7_1"

        def emit_tr(k):
            pn = "t0" if k % 2 == 0 else "t1"
            for c in range(8):
                pe_tr(ps_t[k % 2][:, c, :], pn,
                      xn_tok(k)[:, c * 128:(c + 1) * 128], eye[:, :],
                      start_bank=(c == 0),
                      waits=([f"xn_{k}", "wl_eye"] if c == 0 else []),
                      marks=([f"trA_{k}"] if c == 7 else []))
            act(lambda s, k=k: s.activation(big1[:, :, k * 128:(k + 1) * 128],
                                            ps_t[k % 2][:, :, :], AF.Copy),
                waits=[f"trA_{k}"], marks=[f"trAe_{k}"])
            ps_last[pn] = f"trAe_{k}"

        def g_chunks(i):
            w_total = D - 128 * i
            chunks = []
            off = 0
            while off < w_total:
                chunks.append((off, min(512, w_total - off)))
                off += min(512, w_total - off)
            return chunks

        def emit_G_s(i, s, wait_xn):
            par = i % 2
            pn = "b0" if par == 0 else "b1"
            psg = ps_b[par]
            chunks = g_chunks(i)
            for ci, (off, w) in enumerate(chunks):
                wt = [f"xn_{s}"] if (ci == 0 and wait_xn) else []
                pe_mm(psg[:, off:off + w], pn,
                      xn_tok(s)[:, i * 128:(i + 1) * 128],
                      xn_tok(s)[:, 128 * i + off:128 * i + off + w],
                      start=(s == 0), stop=(s == NT - 1),
                      waits=wt,
                      marks=([f"G_{i}"] if (s == NT - 1 and ci == len(chunks) - 1) else []))

        def emit_G_evac(i):
            par = i % 2
            pn = "b0" if par == 0 else "b1"
            psg = ps_b[par]
            for jj in range(i, 8):
                act(lambda s, i=i, jj=jj, psg=psg:
                    s.activation(Gp[:, gidx[(i, jj)], :],
                                 psg[:, 128 * (jj - i):128 * (jj - i + 1)], AF.Copy),
                    waits=([f"G_{i}"] if jj == i else []),
                    marks=([f"Ge_{i}"] if jj == 7 else []))
            ps_last[pn] = f"Ge_{i}"

        TRD = NT - 8 if NT > 16 else NT
        for k in range(NT):
            ln_stats(x_raw[:, k % 4, :], k)
            if k < 22:
                li_step(k)
            if k >= 1:
                ln_norm(xn_tok(k - 1), x_raw[:, (k - 1) % 4, :], k - 1)
                if k - 1 < TRD:
                    emit_tr(k - 1)
                emit_G_s(0, k - 1, wait_xn=True)
                emit_G_s(1, k - 1, wait_xn=True)
        for j in range(min(NT, 22), 22):
            li_step(j)
        ln_norm(xn_tok(NT - 1), x_raw[:, (NT - 1) % 4, :], NT - 1)
        if NT - 1 < TRD:
            emit_tr(NT - 1)
        emit_G_s(0, NT - 1, wait_xn=True)
        emit_G_s(1, NT - 1, wait_xn=True)
        emit_G_evac(0)
        emit_G_evac(1)
        for i in range(2, 8):
            for s in range(NT):
                emit_G_s(i, s, wait_xn=(s == 0 and i == 2))
            emit_G_evac(i)

        gload(imask, imask_d[:, :], "imask", waits=[f"G_{7}"])

        def mirror_tr(i):
            par = i % 2
            pn = "t0" if par == 0 else "t1"
            nblk = 7 - i
            if nblk == 0:
                return
            for bi, J in enumerate(range(i + 1, 8)):
                pe_tr(ps_t[par][:, bi, :], pn, Gp[:, gidx[(i, J)], :], eye[:, :],
                      start_bank=(bi == 0),
                      waits=([f"Ge_{i}"] if bi == 0 else []),
                      marks=([f"GmT_{i}"] if bi == nblk - 1 else []))
            act(lambda s, i=i, par=par, nblk=nblk:
                s.activation(mscr[:, par, 0:nblk, :], ps_t[par][:, 0:nblk, :], AF.Copy),
                waits=[f"GmT_{i}"], marks=[f"GmE_{i}"])
            ps_last[pn] = f"GmE_{i}"
        mirror_tr(0)
        for i in range(8):
            if i + 1 < 8:
                mirror_tr(i + 1)
            for nb in range(2):
                par = (2 * i + nb) % 2
                pn = "b0" if par == 0 else "b1"
                psh = ps_b[par]
                for J in range(8):
                    lhsT = (Gp[:, gidx[(J, i)], :] if J <= i
                            else mscr[:, i % 2, J - i - 1, :])
                    wt = []
                    if nb == 0 and J == 0:
                        wt = ([f"GmE_{i}", "wl_wk"] if i < 7 else ["wl_wk"])
                    pe_mm(psh[:, 0:512], pn, lhsT,
                          wb0[:, J, nb * 512:(nb + 1) * 512],
                          start=(J == 0), stop=(J == 7),
                          waits=wt, marks=([f"H_{i}_{nb}"] if J == 7 else []))
                act(lambda s, i=i, nb=nb, psh=psh:
                    s.activation(H_sb[:, i, nb * 512:(nb + 1) * 512], psh[:, 0:512],
                                 AF.Copy),
                    waits=[f"H_{i}_{nb}"], marks=[f"He_{i}_{nb}"])
                ps_last[pn] = f"He_{i}_{nb}"
        H_ALL = "H_7_1"
        gload(wb0[:, :, :], wo_d.rearrange("(c p) n -> p c n", p=128), "wo",
              waits=[H_ALL])
        gload(bias_s, bias2_d[:, :], "bias2")

        for al in range(8):
            par = al % 2
            pn = "b0" if par == 0 else "b1"
            psf = ps_b[par]
            for i in range(8):
                for fb in range(2):
                    pe_mm(psf[:, fb * 512:(fb + 1) * 512], pn,
                          wb2[:, i, al * 128:(al + 1) * 128],
                          H_sb[:, i, fb * 512:(fb + 1) * 512],
                          start=(i == 0), stop=(i == 7),
                          waits=(["He_7_1", "wl_wq"]
                                 if (al == 0 and i == 0 and fb == 0) else []),
                          marks=([f"F_{al}"] if (i == 7 and fb == 1) else []))
            dve(lambda v, psf=psf: v.tensor_tensor(Fm, psf[:, :], imask, ALU.mult),
                waits=([f"F_{al}", "wl_imask"] if al == 0 else [f"F_{al}"]))
            dve(lambda v: v.tensor_reduce(R_sb[:, :],
                                          Fm.rearrange("p (a b) -> p a b", a=16),
                                          AX.X, ALU.add),
                marks=[f"R_{al}"])
            ps_last[pn] = f"R_{al}"
            pe_mm(ps_m0[0:16, 2 * al:2 * al + 2], "m0", R_sb[:, :], sel2[:, :],
                  start=(al == 0), stop=(al == 7),
                  waits=([f"R_{al}", "wl_sel2"] if al == 0 else [f"R_{al}"]),
                  marks=[f"ETmm_{al}"])
        dve(lambda v: v.tensor_copy(ET_sb[:, :], ps_m0[0:16, 0:16]),
            waits=["ETmm_7"], marks=["ET_evac"])
        ps_last["m0"] = "ET_evac"

        P.dma("sync", cc_in[:, :], ET_sb[:, :], "s_ccb", waits=["ET_evac"],
              marks=["ccb_in"])
        P.op("gpsimd", lambda g: g.collective_compute(
            "AllReduce", ALU.add, replica_groups=[[0, 1, 2, 3], [4, 5, 6, 7]],
            ins=[cc_in.ap().opt()], outs=[cc_out.ap().opt()]),
            waits=["ccb_in"], incs=[("s_cc", None)])
        P.dma("sync", ETs_sb[:, :], cc_out[:, :], "s_ccb",
              waits=[("val", "s_cc", 1)], marks=["ccb_out"])
        gload(wb2[:, :, :], wvT_d.rearrange("(c p) n -> p c n", p=128), "wvT",
              waits=["ETmm_7"])
        gload(cie[0:16, :], cie_d[:, :], "cie", waits=["ETmm_7"])

        def ln2_stats(T):
            ln_stats(x_raw[:, (NT + T) % 4, :], NT + T)

        def ln2_norm_tr(T):
            tag = NT + T
            war = f"trF_{T - 2}" if T >= 2 else None
            ln_norm(xn2_t[T % 2], x_raw[:, tag % 4, :], tag, war_wait=war)
            parT = T % 2
            pnT = "t0" if parT == 0 else "t1"
            for c in range(8):
                pe_tr(ps_t[parT][:, c, :], pnT, xn2_t[T % 2][:, c * 128:(c + 1) * 128],
                      eye[:, :], start_bank=(c == 0),
                      waits=([f"xn_{tag}"] if c == 0 else []),
                      marks=([f"trF_{T}"] if c == 7 else []))
            ew = [f"trF_{T}"]
            if T >= 2:
                ew.append(f"ffn_{T - 2}_1")
            act(lambda s, parT=parT: s.activation(slabT[:, parT, :, :],
                                                  ps_t[parT][:, :, :], AF.Copy),
                waits=ew, marks=[f"slabTe_{T}"])
            ps_last[pnT] = f"slabTe_{T}"

        def ffn_mm(T):
            parT = T % 2
            for ob in range(2):
                half = ps_b0[:, ob * 512:(ob + 1) * 512]
                pn = f"b0h{ob}"
                pe_mm(half, pn, ones1[:, :],
                      bias_s[:, ob * 512:(ob + 1) * 512],
                      start=True, stop=False,
                      waits=([f"slabTe_{T}", WEFF_DONE, "wl_bias2"] if ob == 0 else []))
                for ic in range(8):
                    pe_mm(half, pn, slabT[:, parT, ic, :],
                          wb1[:, ic, ob * 512:(ob + 1) * 512],
                          start=False, stop=(ic == 7),
                          marks=([f"ffn_{T}_{ob}"] if ic == 7 else []))
                wsil = [f"ffn_{T}_{ob}"]
                if T >= 3:
                    wsil.append(f"p_{T - 3}")
                act(lambda s, T=T, ob=ob: s.activation(
                    g_pool[:, T % 3, ob * 512:(ob + 1) * 512],
                    ps_b0[:, ob * 512:(ob + 1) * 512], AF.Silu),
                    waits=wsil, marks=[f"g_{T}_{ob}"])
                ps_last[pn] = f"g_{T}_{ob}"

        for k in range(TRD, NT):
            emit_tr(k)
        ln2_stats(0)
        ln2_norm_tr(0)
        ln2_stats(1)
        ln2_norm_tr(1)
        ffn_mm(0)
        ln2_stats(2)
        ln2_norm_tr(2)
        ffn_mm(1)
        ln2_stats(3)
        ln2_norm_tr(3)
        ffn_mm(2)

        pe_tr(ps_m1[0:16, 0:16], "m1", ETs_sb[:, :], eye16[:, :], start_bank=True,
              waits=["ccb_out", "wl_eye16"], marks=["Etr"])
        dve(lambda v: v.tensor_copy(E_sb[:, :], ps_m1[0:16, 0:16]), waits=["Etr"],
            marks=["Etr_e"])
        ps_last["m1"] = "Etr_e"
        dve(lambda v: v.reduce_max(sm1[:, :], E_sb[:, :], AX.X))
        dve(lambda v: v.tensor_scalar(sm2[:, :], sm1[:, :], -1.0, None, ALU.mult))
        dve(lambda v: v.tensor_scalar(E_sb[:, :], E_sb[:, :], sm2[:, :], 0.125,
                                      ALU.add, ALU.mult), marks=["smx_in"])
        act(lambda s: s.activation(A_sb[:, :], E_sb[:, :], AF.Exp),
            waits=["smx_in"], marks=["smx_out"])
        dve(lambda v: v.reduce_sum(sm1[:, :], A_sb[:, :], AX.X), waits=["smx_out"])
        dve(lambda v: v.reciprocal(sm2[:, :], sm1[:, :]))
        dve(lambda v: v.tensor_scalar(A_sb[:, :], A_sb[:, :], sm2[:, :], None, ALU.mult))
        dve(lambda v: v.tensor_copy(Ab_sb[:, :], A_sb[:, :]), marks=["A_bf"])
        for nb in range(2):
            pe_mm(ps_m0[0:16, 0:512], "m0", Ab_sb[:, :],
                  cie[0:16, nb * 512:(nb + 1) * 512],
                  start=True, stop=True,
                  waits=(["A_bf", "wl_cie"] if nb == 0 else []),
                  marks=[f"Arep_{nb}"])
            dve(lambda v, nb=nb: v.tensor_copy(Arep_sb[0:16, nb * 512:(nb + 1) * 512],
                                               ps_m0[0:16, 0:512]),
                waits=([f"Arep_{nb}"] + (["He_7_1"] if nb == 0 else [])),
                marks=[f"ArepE_{nb}"])
            ps_last["m0"] = f"ArepE_{nb}"
        for t in range(8):
            par = t % 2
            pn = "b0" if par == 0 else "b1"
            psx = ps_b[par]
            for nb in range(2):
                pe_mm(psx[:, nb * 512:(nb + 1) * 512], pn,
                      cie[0:16, 1024 + t * 128:1024 + (t + 1) * 128],
                      Arep_sb[0:16, nb * 512:(nb + 1) * 512],
                      start=True, stop=True,
                      waits=(["ArepE_1"] if (t == 0 and nb == 0) else []),
                      marks=([f"Bb_{t}"] if nb == 1 else []))
            dve(lambda v, t=t, psx=psx: v.tensor_tensor(M_big[:, t, :], psx[:, :],
                                                        imask, ALU.mult),
                waits=[f"Bb_{t}"], marks=[f"Mb_{t}"])
            ps_last[pn] = f"Mb_{t}"
        for ic in range(8):
            for nb in range(2):
                par = (2 * ic + nb) % 2
                pn = "m0" if par == 0 else "m1"
                psw = ps_m[par]
                for ct in range(8):
                    pe_mm(psw[:, :], pn, wb2[:, ct, ic * 128:(ic + 1) * 128],
                          M_big[:, ct, nb * 512:(nb + 1) * 512],
                          start=(ct == 0), stop=(ct == 7),
                          waits=(["Mb_7", "wl_wvT"]
                                 if (ic == 0 and nb == 0 and ct == 0) else []),
                          marks=([f"wv_{ic}_{nb}"] if ct == 7 else []))
                dve(lambda v, ic=ic, nb=nb, psw=psw:
                    v.tensor_copy(WvE[:, ic, nb * 512:(nb + 1) * 512], psw[:, :]),
                    waits=[f"wv_{ic}_{nb}"], marks=[f"wvE_{ic}_{nb}"])
                ps_last[pn] = f"wvE_{ic}_{nb}"
        WV_ALL = "wvE_7_1"

        SBK = TP // 512
        for cp in range(8):
            for sb in range(SBK):
                par = sb % 2
                pn = "m0" if par == 0 else "m1"
                pso = ps_m[par]
                for ic in range(8):
                    w = []
                    if cp == 0 and sb == 0 and ic == 0:
                        w = [WV_ALL, f"trAe_{NT - 1}"]
                    pe_mm(pso[:, :], pn, WvE[:, ic, cp * 128:(cp + 1) * 128],
                          big1[:, ic, sb * 512:(sb + 1) * 512],
                          start=(ic == 0), stop=(ic == 7),
                          waits=w, marks=([f"op_{cp}_{sb}"] if ic == 7 else []))
                ew = [f"op_{cp}_{sb}"]
                if sb == 0 and cp >= 1:
                    ew += [f"shf_{cp - 1}_{i}" for i in range(4)]
                dve(lambda v, cp=cp, sb=sb, pso=pso:
                    v.tensor_copy(opT[:, sb * 512:(sb + 1) * 512], pso[:, :]),
                    waits=ew, marks=[f"opE_{cp}_{sb}"])
                ps_last[pn] = f"opE_{cp}_{sb}"
            shf_waits = [f"opE_{cp}_{SBK - 1}"]
            if cp == 0:
                shf_waits.append(WV_ALL)
            else:
                shf_waits.append(f"woMM_{cp - 1}_{QCP - 1}_1")
            si = 0
            for hl in range(2):
                for pi in range(2):
                    src = opT.rearrange("p (jh two u) -> p jh two u",
                                        jh=8, two=2)[hl * 64:hl * 64 + 64, :, pi, :]
                    dst = slab[pi * 64:pi * 64 + 64, :].rearrange(
                        "p (t q u) -> p t q u", t=8, q=2)[:, :, hl, :]
                    P.dma("scalarq", dst, src, "s_shf",
                          waits=(shf_waits if (hl == 0 and pi == 0) else []),
                          marks=[f"shf_{cp}_{si}"])
                    si += 1
            SHF = [f"shf_{cp}_{i}" for i in range(4)]
            for q in range(QCP):
                T = QCP * cp + q
                tag = NT + T
                if T + 3 < NTC and T + 3 >= 4:
                    xload(xout_d, T + 3, waits=[f"y_{T - 1}"] if T >= 1 else [f"xn_{NT - 1}"])
                if T + 2 < NTC and T + 2 >= 4:
                    ln2_stats(T + 2)
                if T >= 1:
                    Tp = T - 1
                    dve(lambda v, Tp=Tp: v.tensor_tensor(p_tile[Tp % 2], s_tile[Tp % 2],
                                                         g_pool[:, Tp % 3, :], ALU.mult),
                        waits=[f"sevac_{Tp}_0", f"sevac_{Tp}_1", f"g_{Tp}_0", f"g_{Tp}_1"],
                        marks=[f"p_{Tp}"])
                    wy = [f"p_{Tp}"]
                    if Tp >= 2:
                        wy.append(f"yst_{Tp - 2}")
                    dve(lambda v, Tp=Tp: v.tensor_tensor(
                        y_buf[Tp % 2], p_tile[Tp % 2], x_raw[:, (NT + Tp) % 4, :],
                        ALU.add),
                        waits=wy, marks=[f"y_{Tp}"])
                    P.dma("gpsimd", y_d[Tp * 128:(Tp + 1) * 128, :], y_buf[Tp % 2],
                          "s_yst", waits=[f"y_{Tp}"], marks=[f"yst_{Tp}"])
                if T + 1 < NTC and T + 1 >= 3:
                    ffn_mm(T + 1)
                if T + 2 < NTC and T + 2 >= 4:
                    ln2_norm_tr(T + 2)
                for ob in range(2):
                    half = ps_b1[:, ob * 512:(ob + 1) * 512]
                    pn = f"b1h{ob}"
                    w = []
                    if ob == 0:
                        w = list(SHF)
                        if cp == 0 and q == 0:
                            w.append("Mb_7")
                    pe_mm(half, pn, ones1[:, :],
                          bias_s[:, 1024 + ob * 512:1024 + (ob + 1) * 512],
                          start=True, stop=False, waits=w)
                    for t8 in range(8):
                        pe_mm(half, pn,
                              slab[:, t8 * 2 * UC + q * 128:t8 * 2 * UC + (q + 1) * 128],
                              wb0[:, t8, ob * 512:(ob + 1) * 512],
                              start=False, stop=(t8 == 7),
                              waits=(["wl_wo"]
                                     if (cp == 0 and q == 0 and ob == 0 and t8 == 0) else []),
                              marks=([f"woMM_{cp}_{q}_{ob}"] if t8 == 7 else []))
                    wse = [f"woMM_{cp}_{q}_{ob}"] + ([f"p_{T - 2}"] if T >= 2 else [])
                    act(lambda s, T=T, ob=ob: s.activation(
                        s_tile[T % 2][:, ob * 512:(ob + 1) * 512],
                        ps_b1[:, ob * 512:(ob + 1) * 512], AF.Copy),
                        waits=wse, marks=[f"sevac_{T}_{ob}"])
                    ps_last[pn] = f"sevac_{T}_{ob}"
        Tp = NTC - 1
        dve(lambda v, Tp=Tp: v.tensor_tensor(p_tile[Tp % 2], s_tile[Tp % 2],
                                             g_pool[:, Tp % 3, :], ALU.mult),
            waits=[f"sevac_{Tp}_0", f"sevac_{Tp}_1", f"g_{Tp}_0", f"g_{Tp}_1"],
            marks=[f"p_{Tp}"])
        dve(lambda v, Tp=Tp: v.tensor_tensor(
            y_buf[Tp % 2], p_tile[Tp % 2], x_raw[:, (NT + Tp) % 4, :], ALU.add),
            waits=[f"p_{Tp}", f"yst_{Tp - 2}"], marks=[f"y_{Tp}"])
        P.dma("gpsimd", y_d[Tp * 128:(Tp + 1) * 128, :], y_buf[Tp % 2],
              "s_yst", waits=[f"y_{Tp}"], marks=[f"yst_{Tp}"])

        P.op("gpsimd", lambda s: None,
             waits=[f"yst_{T}" for T in range(NTC - 4, NTC)])

        P.resolve()
        block = stk.enter_context(nc.Block())

        def make_body(name):
            def body(eng):
                for s in P.steps:
                    if ENGMAP[s["stream"]] != name:
                        continue
                    for w in s["waits"]:
                        if isinstance(w, tuple) and w[0] == "val":
                            if w[2] > 0:
                                eng.wait_ge(sems[w[1]], w[2])
                        else:
                            semn, v = P.markv[w]
                            if v > 0:
                                eng.wait_ge(sems[semn], v)
                    sname = CHAINED.get(s["stream"])
                    if sname is not None and s["selfwait"] > 0:
                        eng.wait_ge(sems[sname], s["selfwait"])
                    inst = s["fn"](eng)
                    if inst is None:
                        continue
                    for semn, n in s["incs"]:
                        if n:
                            inst.then_inc(sems[semn], n)
                        else:
                            inst.then_inc(sems[semn])
            return body

        block.tensor(make_body("tensor"))
        block.vector(make_body("vector"))
        block.scalar(make_body("scalar"))
        block.gpsimd(make_body("gpsimd"))
        block.sync(make_body("sync"))

    return nc



def _consts():
    idrep = np.zeros((16, 1024), BF)
    for a in range(16):
        idrep[a, a * 64:(a + 1) * 64] = 1
    e2 = np.zeros((16, 1024), BF)
    for t in range(8):
        for beta in range(2):
            e2[2 * t + beta, t * 128 + beta * 64:t * 128 + (beta + 1) * 64] = 1
    cAv = np.concatenate([idrep, e2], axis=1)
    im = np.zeros((128, 1024), BF)
    for p in range(128):
        for h in range(16):
            im[p, h * 64 + (p % 64)] = 1
    eye = np.eye(128, dtype=BF)
    eye16 = np.eye(16, dtype=np.float32)
    sel2 = np.zeros((128, 2), np.float32)
    sel2[0:64, 0] = 1
    sel2[64:128, 1] = 1
    return cAv, im, eye, eye16, sel2


def kernel(L0_attn_x, L0_attn_Wq, L0_attn_Wk, L0_attn_Wv, L0_attn_Wo, L0_attn_bo,
           L0_ffn_W, L0_ffn_b, L0_ffn_lora_down, L0_ffn_lora_up,
           L0_ffn_down_aux, L0_ffn_up_aux):
    x = np.ascontiguousarray(np.asarray(L0_attn_x, np.float32))
    N, S, _ = x.shape
    TP = S // 4
    UC = TP // 16
    U = S // 16
    cAv, im, eye, eye16, sel2 = _consts()
    bias2 = np.zeros((1, 2048), BF)
    bias2[0, 0:1024] = np.asarray(L0_ffn_b, np.float32).astype(BF)
    bias2[0, 1024:2048] = np.asarray(L0_attn_bo, np.float32).astype(BF)
    common = dict(
        wq=np.ascontiguousarray(np.asarray(L0_attn_Wq, np.float32)),
        wk=np.ascontiguousarray(np.asarray(L0_attn_Wk, np.float32)),
        wvT=np.ascontiguousarray(np.asarray(L0_attn_Wv, np.float32).T),
        wo=np.ascontiguousarray(np.asarray(L0_attn_Wo, np.float32)),
        wT=np.ascontiguousarray(np.asarray(L0_ffn_W, np.float32).T),
        luT=np.ascontiguousarray(np.asarray(L0_ffn_lora_up, np.float32).T),
        ld=np.ascontiguousarray(np.asarray(L0_ffn_lora_down, np.float32)),
        uaT=np.ascontiguousarray(np.asarray(L0_ffn_up_aux, np.float32).T),
        da=np.ascontiguousarray(np.asarray(L0_ffn_down_aux, np.float32)),
        cie=cAv, bias2=bias2, imask=im, eye=eye, eye16=eye16, sel2=sel2,
    )
    in_maps = []
    for core in range(NC):
        n, c = divmod(core, 4)
        xt = np.ascontiguousarray(
            x[n, c * TP:(c + 1) * TP].reshape(UC, 16, D).transpose(1, 0, 2)
            .reshape(TP, D))
        xo = np.ascontiguousarray(
            x[n].reshape(16, U, D)[:, c * UC:(c + 1) * UC, :].reshape(TP, D))
        in_maps.append(dict(common, xtok=xt, xout=xo))
    nc_prog = build_program(TP)
    res = run_bass_kernel_spmd(nc_prog, in_maps, core_ids=list(range(NC)))
    out = np.empty_like(x)
    for core in range(NC):
        n, c = divmod(core, 4)
        y = np.asarray(res.results[core]["y"], np.float32)
        out[n].reshape(16, U, D)[:, c * UC:(c + 1) * UC, :] = y.reshape(16, UC, D)
    return out
